# version 11
# baseline (speedup 1.0000x reference)
import os
"""Fused megakernel for mk_macaron_d1 (retnet mixer + hadamard-SwiGLU FFN).

One raw-Bass program per NeuronCore, single run_bass_kernel_spmd launch.
Sharding: 16384 tokens -> 8 cores x 2048 tokens (+256-token lookback for the
EMA scan; decay=sigmoid(0.1*N(0,1)) so d^256 < 1e-38 — exact at fp32/fp64).

Layout: feature-major ([features on partitions, tokens on free]).
FWHT-2048 on activations: per-tile (H128*diag(SU)) TensorE matmul, then 4
cross-tile butterfly stages on DVE (bf16).  FWHT-8192 folded into W_down on
device once (f32 butterflies + H4-folded H128 matmuls), along with SU_dp and
the up-scale; all other element scales folded into per-partition psum-evac
scales.  EMA scan = one tensor_tensor_scan per 128-channel tile over the full
2304-token window.  bf16 matmul operands / f32 accumulation / f32 elementwise
(this problem's stated precision bar).
"""
import numpy as np
from contextlib import ExitStack

import concourse.bass as bass
import concourse.mybir as mybir
from concourse.bass_utils import run_bass_kernel_spmd

F32 = mybir.dt.float32
BF16 = mybir.dt.bfloat16
AF = mybir.ActivationFunctionType
ALU = mybir.AluOpType

NCORES = 8
D, I = 2048, 8192
DG, IG = 16, 64
LB = 256
WIN = 2304
WOUT = 2048
A_TILES = [(0, 512), (512, 512), (1024, 512), (1536, 512), (2048, 256)]
D_TILES = [(256, 512), (768, 512), (1280, 512), (1792, 512)]
C_Z = float(1.0 / (64.0 * np.sqrt(2048.0)))
EPS = 1e-5
ROLES = ("sync", "gpsimd", "vector", "scalar", "tensor")
H4SIGN = [[1, 1, 1, 1], [1, -1, 1, -1], [1, 1, -1, -1], [1, -1, -1, 1]]


def build(upto=None):
    import os
    if upto is None:
        upto = os.environ.get("K_UPTO", "d2")
    stages = ["p0", "pre", "a", "bc", "d1", "d2"]
    lim = stages.index(upto)

    def want(name):
        return stages.index(name) <= lim

    nc = bass.Bass("TRN2", target_bir_lowering=False)

    x_img = nc.dram_tensor("x_img", [128, 16, WIN], F32, kind="ExternalInput")
    wup_img = nc.dram_tensor("wup_img", [64, 128, 2048], F32, kind="ExternalInput")
    wgt_img = nc.dram_tensor("wgt_img", [64, 128, 2048], F32, kind="ExternalInput")
    wdn_img = nc.dram_tensor("wdn_img", [16, 128, 8192], F32, kind="ExternalInput")
    win_img = nc.dram_tensor("win_img", [128, 16, 2048], F32, kind="ExternalInput")
    wout_img = nc.dram_tensor("wout_img", [128, 16, 2048], F32, kind="ExternalInput")
    dec_img = nc.dram_tensor("dec_img", [128, 16], F32, kind="ExternalInput")
    bias_img = nc.dram_tensor("bias_img", [128, 16], F32, kind="ExternalInput")
    wsu_img = nc.dram_tensor("wsu_img", [128, 64], F32, kind="ExternalInput")
    wsg_img = nc.dram_tensor("wsg_img", [128, 64], F32, kind="ExternalInput")
    wdp_img = nc.dram_tensor("wdp_img", [128, 16], F32, kind="ExternalInput")
    su_img = nc.dram_tensor("su_img", [128, 16], F32, kind="ExternalInput")
    sdp_img = nc.dram_tensor("sdp_img", [128, 64], F32, kind="ExternalInput")
    h128_img = nc.dram_tensor("h128_img", [128, 128], F32, kind="ExternalInput")
    out_img = nc.dram_tensor("out_img", [128, 16, WOUT], F32, kind="ExternalOutput")
    _k = dict(kind="ExternalOutput") if os.environ.get("K_DBG") else {}
    wup2 = nc.dram_tensor("wup2", [64, 128, 2048], BF16, **_k)
    wgt2 = nc.dram_tensor("wgt2", [64, 128, 2048], BF16, **_k)
    wdn2 = nc.dram_tensor("wdn2", [8, 64, 128, 256], BF16, **_k)
    x1s = nc.dram_tensor("x1s", [128, 16, WIN], F32, **_k)
    ln2s = nc.dram_tensor("ln2s", [128, 16, WIN], BF16, **_k)
    x2s = nc.dram_tensor("x2s", [128, 16, WOUT], F32, **_k)
    dbg_on = bool(os.environ.get("K_DBG"))
    if dbg_on:
        dbg_proj = nc.dram_tensor("dbg_proj", [128, 4, WIN], F32, kind="ExternalOutput")
        dbg_scan = nc.dram_tensor("dbg_scan", [128, 16, WIN], BF16, kind="ExternalOutput")

    C = {k: 0 for k in ("ld", "st", "sc", "gw", "gpc", "pe", "act", "dve",
                        "bar", "lddn0", "lddn1", "ldln0", "ldln1",
                        "sc0", "sc1", "gw0", "gw1", "gwb", "gwb0", "gwb1",
                        "gww")}

    es = ExitStack()
    with es:
        S = {k: es.enter_context(nc.semaphore(k)) for k in C}
        ps8 = es.enter_context(nc.psum_tensor("ps8", [128, 8, 512], F32))
        dmat = es.enter_context(nc.sbuf_tensor("dmat", [128, 16], F32))
        omd = es.enter_context(nc.sbuf_tensor("omd", [128, 16], F32))
        wdp05 = es.enter_context(nc.sbuf_tensor("wdp05", [128, 16], F32))
        biasc = es.enter_context(nc.sbuf_tensor("biasc", [128, 16], F32))
        sg64 = es.enter_context(nc.sbuf_tensor("sg64", [128, 64], F32))
        dwnsc = es.enter_context(nc.sbuf_tensor("dwnsc", [128, 64], F32))
        hsub = es.enter_context(nc.sbuf_tensor("hsub", [128, 16, 128], BF16))
        Hb = es.enter_context(nc.sbuf_tensor("Hb", [128, 128], BF16))
        Hnb = es.enter_context(nc.sbuf_tensor("Hnb", [128, 128], BF16))
        ones_colb = es.enter_context(nc.sbuf_tensor("ones_colb", [128, 1], BF16))
        ones1f = es.enter_context(nc.sbuf_tensor("ones1f", [1, 128], F32))
        block = es.enter_context(nc.Block())

        def inc(sem):
            C[sem] += 1
            return C[sem]

        def dinc(sem):
            C[sem] += 16
            return C[sem]

        def run_phase(fn):
            base = dict(C)
            ends = []
            for role in ROLES:
                C.clear()
                C.update(base)

                def _f(eng, _r=role):
                    fn(_r, eng)

                getattr(block, role)(_f)
                ends.append(dict(C))
            for e2 in ends:
                assert e2 == ends[0], f"counter divergence {ends}"
            C.clear()
            C.update(ends[0])

        def barrier(role, e):
            if role == "sync":
                for k in ("ld", "st", "lddn0", "lddn1", "ldln0", "ldln1"):
                    if C[k]:
                        e.wait_ge(S[k], C[k])
            elif role == "scalar":
                for k in ("sc", "sc0", "sc1", "act"):
                    if C[k]:
                        e.wait_ge(S[k], C[k])
            elif role == "gpsimd":
                for k in ("gw", "gw0", "gw1", "gwb", "gwb0", "gwb1", "gww",
                          "gpc"):
                    if C[k]:
                        e.wait_ge(S[k], C[k])
            elif role == "vector":
                if C["dve"]:
                    e.wait_ge(S["dve"], C["dve"])
            elif role == "tensor":
                if C["pe"]:
                    e.wait_ge(S["pe"], C["pe"])
            C["bar"] += 5
            tgt = C["bar"]
            e.nop().then_inc(S["bar"], 1)
            e.wait_ge(S["bar"], tgt)

        p0es = ExitStack()
        dec_r = p0es.enter_context(nc.sbuf_tensor("dec_r", [128, 16], F32))
        wsu_r = p0es.enter_context(nc.sbuf_tensor("wsu_r", [128, 64], F32))
        wsg_r = p0es.enter_context(nc.sbuf_tensor("wsg_r", [128, 64], F32))
        wdp_r = p0es.enter_context(nc.sbuf_tensor("wdp_r", [128, 16], F32))
        su_r = p0es.enter_context(nc.sbuf_tensor("su_r", [128, 16], F32))
        sdp_r = p0es.enter_context(nc.sbuf_tensor("sdp_r", [128, 64], F32))
        h128f = p0es.enter_context(nc.sbuf_tensor("h128f", [128, 128], F32))

        def p0(role, e):
            loads = [(dec_r, dec_img), (wsu_r, wsu_img), (wsg_r, wsg_img),
                     (wdp_r, wdp_img), (su_r, su_img), (sdp_r, sdp_img),
                     (h128f, h128_img), (biasc, bias_img)]
            for dst, src in loads:
                t = dinc("ld")
                if role == "sync":
                    e.dma_start(dst[:], src[:, :]).then_inc(S["ld"], 16)
            t_ld = C["ld"]
            if role == "gpsimd":
                e.memset(ones_colb[:], 1.0)
                e.memset(ones1f[:], 1.0).then_inc(S["gpc"], 1)
            C["gpc"] += 1
            sa0 = inc("act")
            if role == "scalar":
                e.wait_ge(S["ld"], t_ld)
                e.activation(dmat[:], dec_r[:], AF.Sigmoid).then_inc(S["act"], 1)
            sv0 = None
            if True:
                if role == "vector":
                    e.wait_ge(S["act"], sa0)
                    e.tensor_scalar(omd[:], dmat[:], -1.0, 1.0, ALU.mult, ALU.add)
                    e.tensor_scalar(wdp05[:], wdp_r[:], 0.5, None, ALU.mult)
                    e.tensor_scalar(sg64[:], wsg_r[:], 64.0 * C_Z, None, ALU.mult)
                    e.tensor_tensor(dwnsc[:], wsu_r[:], sdp_r[:], ALU.mult)
                    e.tensor_scalar(dwnsc[:], dwnsc[:],
                                    0.7071067811865476 * C_Z, None, ALU.mult)
                    e.tensor_copy(Hb[:], h128f[:])
                    e.tensor_scalar(Hnb[:], h128f[:], -1.0, None, ALU.mult)
                    for g in range(16):
                        ins = e.tensor_scalar(hsub[:, g], h128f[:],
                                              su_r[:, g:g + 1], None, ALU.mult)
                        if g == 15:
                            ins.then_inc(S["dve"], 1)
                sv0 = inc("dve")
            barrier(role, e)

        run_phase(p0)
        p0es.close()
        if not want("pre"):
            return nc

        prees = ExitStack()
        wdn_in = prees.enter_context(
            nc.sbuf_tensor("wdn_in", [128, 2, 64, 128], F32))
        wdn_pg = prees.enter_context(nc.sbuf_tensor("wdn_pg", [128, 64, 128], BF16))
        wdn_pg2 = prees.enter_context(
            nc.sbuf_tensor("wdn_pg2", [128, 64, 128], BF16))
        wdn_b = prees.enter_context(
            nc.sbuf_tensor("wdn_b", [128, 2, 64, 128], BF16))
        wdn_o = prees.enter_context(nc.sbuf_tensor("wdn_o", [128, 64, 256], BF16))

        def pre(role, e):
            p0_dve = C["dve"]
            dve_hist = {}
            pe_hist = {}
            prev_act_done = 0
            t_chunk = {}
            sa_marks = {}
            for s_ in range(16):
                hd = s_ % 2
                par = s_ % 2
                gsem = "gw0" if par == 0 else "gw1"
                t_chunk[s_] = (gsem, dinc(gsem))
                if role == "sync":
                    if s_ >= 2:
                        e.wait_ge(S["dve"], dve_hist[s_ - 2])
                    e.dma_start(wdn_in[:, par],
                                wdn_img[s_].rearrange("p (o d) -> p o d", o=64)
                                ).then_inc(S[gsem], 16)
                groups = [] if s_ == 0 else ([s_ - 1] if s_ < 15 else [14, 15])
                for grp in groups:
                    bsem = "gwb0" if grp % 2 == 0 else "gwb1"
                    bval = C[bsem]
                    C[bsem] += 32
                    if role == "gpsimd":
                        if grp >= 2:
                            e.wait_ge(S[bsem], bval)
                        e.dma_start(wup2[4 * grp:4 * grp + 4],
                                    wup_img[4 * grp:4 * grp + 4]
                                    ).then_inc(S[bsem], 16)
                        e.dma_start(wgt2[4 * grp:4 * grp + 4],
                                    wgt_img[4 * grp:4 * grp + 4]
                                    ).then_inc(S[bsem], 16)
                if role == "vector":
                    e.wait_ge(S[gsem], t_chunk[s_][1])
                    if s_ >= 2:
                        e.wait_ge(S["pe"], pe_hist[s_ - 2])
                    win_ = wdn_in[:, par]
                    e.tensor_tensor(wdn_pg[:, 0:32], win_[:, 0:32],
                                    win_[:, 32:64], ALU.add)
                    e.tensor_tensor(wdn_pg[:, 32:64], win_[:, 0:32],
                                    win_[:, 32:64], ALU.subtract)
                    for (src, dst, a, dlt) in ((wdn_pg, wdn_pg2, 2, 16),
                                               (wdn_pg2, wdn_pg, 4, 8)):
                        vs = src.rearrange("p (a b c) d -> p a b c d", a=a, b=2, c=dlt)
                        vd = dst.rearrange("p (a b c) d -> p a b c d", a=a, b=2, c=dlt)
                        e.tensor_tensor(vd[:, :, 0], vs[:, :, 0], vs[:, :, 1], ALU.add)
                        e.tensor_tensor(vd[:, :, 1], vs[:, :, 0], vs[:, :, 1],
                                        ALU.subtract)
                    vs = wdn_pg.rearrange("p (a b c) d -> p a b c d", a=8, b=2, c=4)
                    vd = wdn_b[:, par].rearrange("p (a b c) d -> p a b c d",
                                                 a=8, b=2, c=4)
                    e.tensor_tensor(vd[:, :, 0], vs[:, :, 0], vs[:, :, 1], ALU.add)
                    e.tensor_tensor(vd[:, :, 1], vs[:, :, 0], vs[:, :, 1],
                                    ALU.subtract).then_inc(S["dve"], 1)
                dve_hist[s_] = inc("dve")
                for g2 in range(16):
                    q = g2 % 2
                    if role == "tensor":
                        if g2 == 0:
                            e.wait_ge(S["dve"], dve_hist[s_])
                            if s_ == 0:
                                e.wait_ge(S["dve"], p0_dve)
                        if g2 >= 2 or s_ > 0:
                            w = prev_act_done if g2 < 2 else sa_marks[g2 - 2]
                            e.wait_ge(S["act"], w)
                        for j in range(4):
                            for k in range(4):
                                lhs = Hb if H4SIGN[j][k] > 0 else Hnb
                                ins = e.matmul(ps8[:, q * 4 + j, 0:128], lhs[:],
                                               wdn_b[:, par, 4 * g2 + k, :],
                                               start=(k == 0), stop=(k == 3))
                                if j == 3 and k == 3:
                                    ins.then_inc(S["pe"], 1)
                    sp = inc("pe")
                    if role == "scalar":
                        e.wait_ge(S["pe"], sp)
                        e.activation(
                            wdn_o[:, 4 * g2:4 * g2 + 4, hd * 128:(hd + 1) * 128],
                            ps8[:, q * 4:q * 4 + 4, 0:128], AF.Copy
                            ).then_inc(S["act"], 1)
                    sa_marks[g2] = inc("act")
                pe_hist[s_] = C["pe"]
                prev_act_done = C["act"]
                if hd == 1:
                    cc = s_ // 2
                    for o0 in range(0, 64, 16):
                        t = dinc("sc")
                        if role == "scalar":
                            e.dma_start(
                                wdn2[cc, o0:o0 + 16].rearrange("o p d -> p o d"),
                                wdn_o[:, o0:o0 + 16, :]).then_inc(S["sc"], 16)
                    t_sc = C["sc"]
                    if role == "scalar":
                        e.wait_ge(S["sc"], t_sc)
            barrier(role, e)

        if want("pre"):
            run_phase(pre)
        prees.close()
        if not want("a"):
            return nc

        class LNPipe:

            def __init__(self, stat, rows, tsub, src, dst, T):
                self.stat, self.rows, self.tsub = stat, rows, tsub
                self.src, self.dst, self.T = src, dst, T
                self.pe_marks = {}
                self.sv_stats = {}

            def statprep(self, role, e, q, guards=()):
                T = self.T
                if role == "vector":
                    for sem, val in guards:
                        e.wait_ge(S[sem], val)
                    if q >= 1:
                        e.wait_ge(S["pe"], self.pe_marks[q - 1])
                    e.tensor_copy(self.stat[:, 0, :, 0:T],
                                  self.src[:, 4 * q:4 * q + 4, 0:T])
                    e.tensor_tensor(self.stat[:, 1, :, 0:T],
                                    self.src[:, 4 * q:4 * q + 4, 0:T],
                                    self.src[:, 4 * q:4 * q + 4, 0:T], ALU.mult
                                    ).then_inc(S["dve"], 1)
                self.sv_stats[q] = inc("dve")

            def statmm(self, role, e, q, pe_guards=()):
                T = self.T
                if role == "tensor":
                    e.wait_ge(S["dve"], self.sv_stats[q])
                    for sem, val in pe_guards:
                        e.wait_ge(S[sem], val)
                    for j in range(4):
                        e.matmul(ps8[0:1, 7, 0:T], ones_colb[:, 0:1],
                                 self.stat[:, 0, j, 0:T],
                                 start=(q == 0 and j == 0),
                                 stop=(q == 3 and j == 3))
                        ins = e.matmul(ps8[0:1, 0, 0:T], ones_colb[:, 0:1],
                                       self.stat[:, 1, j, 0:T],
                                       start=(q == 0 and j == 0),
                                       stop=(q == 3 and j == 3))
                    ins.then_inc(S["pe"], 1)
                self.pe_marks[q] = inc("pe")

            def rows1(self, role, e):
                T, rows = self.T, self.rows
                if role == "vector":
                    e.wait_ge(S["pe"], self.pe_marks[3])
                    e.tensor_scalar(rows[0:1, 0, 0:T], ps8[0:1, 7, 0:T],
                                    1.0 / 2048.0, None, ALU.mult)
                    e.tensor_scalar(rows[0:1, 1, 0:T], ps8[0:1, 0, 0:T],
                                    1.0 / 2048.0, None, ALU.mult)
                    e.tensor_tensor(rows[0:1, 2, 0:T], rows[0:1, 0, 0:T],
                                    rows[0:1, 0, 0:T], ALU.mult)
                    e.tensor_tensor(rows[0:1, 1, 0:T], rows[0:1, 1, 0:T],
                                    rows[0:1, 2, 0:T], ALU.subtract)
                    e.tensor_scalar(rows[0:1, 2, 0:T], rows[0:1, 1, 0:T],
                                    EPS, None, ALU.add).then_inc(S["dve"], 1)
                self.sv_var = inc("dve")

            def sqrt(self, role, e):
                T, rows = self.T, self.rows
                if role == "scalar":
                    e.wait_ge(S["dve"], self.sv_var)
                    e.activation(rows[0:1, 1, 0:T], rows[0:1, 2, 0:T], AF.Sqrt
                                 ).then_inc(S["act"], 1)
                self.sa_sq = inc("act")

            def rows2(self, role, e):
                T, rows = self.T, self.rows
                if role == "vector":
                    e.wait_ge(S["act"], self.sa_sq)
                    e.reciprocal(rows[0:1, 2, 0:T], rows[0:1, 1, 0:T])
                    e.tensor_tensor(rows[0:1, 1, 0:T], rows[0:1, 0, 0:T],
                                    rows[0:1, 2, 0:T], ALU.mult
                                    ).then_inc(S["dve"], 1)
                self.sv_rows = inc("dve")

            def bcast(self, role, e):
                T, rows = self.T, self.rows
                if role == "tensor":
                    e.wait_ge(S["dve"], self.sv_rows)
                    e.matmul(ps8[:, 7, 0:T], ones1f[0:1, :], rows[0:1, 1, 0:T],
                             start=True, stop=True)
                    e.matmul(ps8[:, 0, 0:T], ones1f[0:1, :], rows[0:1, 2, 0:T],
                             start=True, stop=True).then_inc(S["pe"], 1)
                self.sp_bc = inc("pe")

            def apply(self, role, e, r0, r1, dst=None):
                T = self.T
                dst = self.dst if dst is None else dst
                if role == "vector":
                    if r0 == 0:
                        e.wait_ge(S["pe"], self.sp_bc)
                    mb = ps8[:, 7:8, 0:T].broadcast_to([128, 2, T])
                    rb = ps8[:, 0:1, 0:T].broadcast_to([128, 2, T])
                    for q8 in range(r0, r1):
                        e.tensor_tensor(self.tsub[:, 0:1, 0:T],
                                        self.src[:, 2 * q8:2 * q8 + 1, 0:T],
                                        mb[:, 0:1], ALU.subtract)
                        e.tensor_tensor(dst[:, 2 * q8:2 * q8 + 1, 0:T],
                                        self.tsub[:, 0:1, 0:T], rb[:, 0:1],
                                        ALU.mult)
                        e.tensor_tensor(self.tsub[:, 0:1, 0:T],
                                        self.src[:, 2 * q8 + 1:2 * q8 + 2, 0:T],
                                        mb[:, 0:1], ALU.subtract)
                        ins = e.tensor_tensor(
                            dst[:, 2 * q8 + 1:2 * q8 + 2, 0:T],
                            self.tsub[:, 0:1, 0:T], rb[:, 0:1], ALU.mult)
                        if q8 == r1 - 1:
                            ins.then_inc(S["dve"], 1)
                self.sv_apply = inc("dve")

        def emit_h128_grp(role, e, lnb, fwb, T, sv_lnapply, st8, g0):
            for g in range(g0, g0 + 4):
                slot = 7 if g % 2 == 0 else 0
                if role == "tensor":
                    if g == 0:
                        e.wait_ge(S["dve"], sv_lnapply)
                    if g >= 2:
                        e.wait_ge(S["act"], st8["sa_z"][g - 2])
                    e.matmul(ps8[:, slot, 0:T], hsub[:, g], lnb[:, g, 0:T],
                             start=True, stop=True).then_inc(S["pe"], 1)
                st8["sp_z"][g] = inc("pe")
                if role == "scalar":
                    e.wait_ge(S["pe"], st8["sp_z"][g])
                    e.activation(fwb[:, g, 0:T], ps8[:, slot, 0:T], AF.Copy
                                 ).then_inc(S["act"], 1)
                st8["sa_z"][g] = inc("act")

        def emit_butterflies(role, e, lnb, fwb, T, sa_z15):
            if role == "vector":
                e.wait_ge(S["act"], sa_z15)
                for i, (src, dst, dlt) in enumerate(
                        ((fwb, lnb, 8), (lnb, fwb, 4), (fwb, lnb, 2), (lnb, fwb, 1))):
                    a = 16 // (2 * dlt)
                    vs = src.rearrange("p (a b c) t -> p a b c t", a=a, b=2, c=dlt)
                    vd = dst.rearrange("p (a b c) t -> p a b c t", a=a, b=2, c=dlt)
                    e.tensor_tensor(vd[:, :, 0, :, 0:T], vs[:, :, 0, :, 0:T],
                                    vs[:, :, 1, :, 0:T], ALU.add)
                    ins = e.tensor_tensor(vd[:, :, 1, :, 0:T], vs[:, :, 0, :, 0:T],
                                          vs[:, :, 1, :, 0:T], ALU.subtract)
                    if i == 3:
                        ins.then_inc(S["dve"], 1)
            return inc("dve")

        def emit_ffn(role, e, zf, h, silu_b, x_t, wug_s, wdn_s, T,
                     sv_zf, gate_stash=False, loop1_hook=None, loop2_hook=None,
                     mid_guard_cell=None):
            sp_up = {}
            sp_gt = {}
            sa_silu = {}
            sv_mid = {}
            t_slab = {}
            gwb_tot = {"gwb0": C["gwb0"], "gwb1": C["gwb1"]}

            def slab_sem(j):
                return "sc0" if j % 2 == 0 else "sc1"

            def slab_load(j):
                sem = slab_sem(j)
                t1 = dinc(sem)
                t2 = dinc(sem)
                if role == "scalar":
                    if gate_stash:
                        g = j // 4
                        bs = "gwb0" if g % 2 == 0 else "gwb1"
                        e.wait_ge(S[bs], min(32 * (g // 2 + 1), gwb_tot[bs]))
                    pbj = j % 2
                    e.dma_start(wug_s[:, 0, pbj, :], wup2[j]
                                ).then_inc(S[sem], 16)
                    e.dma_start(wug_s[:, 1, pbj, :], wgt2[j]
                                ).then_inc(S[sem], 16)
                t_slab[j] = t2

            slab_load(0)
            slab_load(1)
            for it in range(64):
                pb = it % 2
                s_up = 1 if pb == 0 else 3
                s_gt = 2 if pb == 0 else 4
                if role == "tensor":
                    if it == 0:
                        e.wait_ge(S["dve"], sv_zf)
                    e.wait_ge(S[slab_sem(it)], t_slab[it])
                    if it >= 2:
                        e.wait_ge(S["dve"], sv_mid[it - 2])
                        e.wait_ge(S["act"], sa_silu[it - 2])
                    for dg in range(16):
                        ins = e.matmul(ps8[:, s_up, 0:T], wug_s[:, 0, pb,
                                       dg * 128:(dg + 1) * 128], zf[:, dg, 0:T],
                                       start=(dg == 0), stop=(dg == 15))
                        if dg == 15:
                            ins.then_inc(S["pe"], 1)
                sp_up[it] = inc("pe")
                if role == "tensor":
                    for dg in range(16):
                        ins = e.matmul(ps8[:, s_gt, 0:T], wug_s[:, 1, pb,
                                       dg * 128:(dg + 1) * 128], zf[:, dg, 0:T],
                                       start=(dg == 0), stop=(dg == 15))
                        if dg == 15:
                            ins.then_inc(S["pe"], 1)
                sp_gt[it] = inc("pe")
                if role == "scalar":
                    e.wait_ge(S["pe"], sp_gt[it])
                if it + 2 < 64:
                    slab_load(it + 2)
                if role == "scalar":
                    e.wait_ge(S["pe"], sp_gt[it])
                    if it >= 2:
                        e.wait_ge(S["dve"], sv_mid[it - 2])
                    e.activation(silu_b[:, pb, 0:T], ps8[:, s_gt, 0:T], AF.Silu,
                                 scale=sg64[:, it:it + 1]).then_inc(S["act"], 1)
                sa_silu[it] = inc("act")
                if role == "vector":
                    e.wait_ge(S["act"], sa_silu[it])
                    if it == 48 and mid_guard_cell and "mg" in mid_guard_cell:
                        mg = mid_guard_cell["mg"]
                        e.wait_ge(S[mg[0]], mg[1])
                    e.scalar_tensor_tensor(h[:, it, 0:T], ps8[:, s_up, 0:T],
                                           dwnsc[:, it:it + 1],
                                           silu_b[:, pb, 0:T],
                                           ALU.mult, ALU.mult
                                           ).then_inc(S["dve"], 1)
                sv_mid[it] = inc("dve")
                if loop1_hook is not None:
                    loop1_hook(role, e, it)
            pp_done = {}
            sv_res = {}
            t_dnld = {}
            for c8 in range(8):
                dt0, dt1 = 2 * c8, 2 * c8 + 1
                sA, sB = (5, 6) if c8 % 2 == 0 else (1, 2)
                for p8 in range(8):
                    idx = c8 * 8 + p8
                    pc = idx % 2
                    dn_sem = "lddn0" if pc == 0 else "lddn1"
                    if role == "sync":
                        if idx >= 2:
                            e.wait_ge(S["pe"], pp_done[idx - 2])
                    t_dnld[idx] = (dn_sem, dinc(dn_sem))
                    if role == "sync":
                        e.dma_start(
                            wdn_s[:, pc],
                            wdn2[c8, p8 * 8:(p8 + 1) * 8].rearrange(
                                "o p d -> p o d")).then_inc(S[dn_sem], 16)
                    if role == "tensor":
                        if c8 == 0 and p8 == 0:
                            e.wait_ge(S["dve"], sv_mid[63])
                        if c8 >= 2 and p8 == 0:
                            e.wait_ge(S["dve"], sv_res[c8 - 2])
                        e.wait_ge(S[t_dnld[idx][0]], t_dnld[idx][1])
                        for ol in range(8):
                            o = p8 * 8 + ol
                            e.matmul(ps8[:, sA, 0:T], wdn_s[:, pc, ol, 0:128],
                                     h[:, o, 0:T], start=(o == 0), stop=(o == 63))
                            ins = e.matmul(ps8[:, sB, 0:T],
                                           wdn_s[:, pc, ol, 128:256],
                                           h[:, o, 0:T], start=(o == 0),
                                           stop=(o == 63))
                            if ol == 7:
                                ins.then_inc(S["pe"], 1)
                    pp_done[idx] = inc("pe")
                if role == "vector":
                    e.wait_ge(S["pe"], pp_done[c8 * 8 + 7])
                    e.scalar_tensor_tensor(x_t[:, dt0, 0:T], ps8[:, sA, 0:T],
                                           wdp05[:, dt0:dt0 + 1],
                                           x_t[:, dt0, 0:T], ALU.mult, ALU.add)
                    e.scalar_tensor_tensor(x_t[:, dt1, 0:T], ps8[:, sB, 0:T],
                                           wdp05[:, dt1:dt1 + 1],
                                           x_t[:, dt1, 0:T], ALU.mult, ALU.add
                                           ).then_inc(S["dve"], 1)
                sv_res[c8] = inc("dve")
                if loop2_hook is not None:
                    loop2_hook(role, e, c8, sv_res)
            return sv_res[7]

        aes = ExitStack()
        x_t = aes.enter_context(nc.sbuf_tensor("x_t", [128, 2, 16, 512], F32))
        lnb = aes.enter_context(nc.sbuf_tensor("lnb", [128, 16, 512], BF16))
        fwb = aes.enter_context(nc.sbuf_tensor("fwb", [128, 16, 512], BF16))
        h_b = aes.enter_context(nc.sbuf_tensor("h_b", [128, 64, 512], BF16))
        stat = aes.enter_context(nc.sbuf_tensor("stat", [128, 2, 4, 512], BF16))
        tsub = aes.enter_context(nc.sbuf_tensor("tsub", [128, 1, 512], F32))
        silu_b = aes.enter_context(nc.sbuf_tensor("silu_b", [128, 2, 512], BF16))
        rows = aes.enter_context(nc.sbuf_tensor("rows", [1, 3, 512], F32))
        wug_s = aes.enter_context(nc.sbuf_tensor("wug_s", [128, 2, 2, 2048], BF16))
        wdn_s = aes.enter_context(nc.sbuf_tensor("wdn_s", [128, 2, 8, 256], BF16))

        def make_front(src, dst_ln, zbuf, T, t0, img, xg_fn, ln, st8, cell,
                       statmm_g0_fn, statprep_g0_fn):
            def s_load(role, e):
                t = dinc("ld")
                if role == "sync":
                    for sem, val in xg_fn():
                        e.wait_ge(S[sem], val)
                    e.dma_start(src[:, :, 0:T], img[:, :, t0:t0 + T]
                                ).then_inc(S["ld"], 16)
                cell["t_x"] = t
            def mk_q(q):
                def s_q(role, e):
                    g = [("ld", cell["t_x"])] + (statprep_g0_fn() if q == 0
                                                 else [])
                    ln.statprep(role, e, q, guards=g)
                    ln.statmm(role, e, q,
                              pe_guards=(statmm_g0_fn() if q == 0 else []))
                return s_q
            def s_rows1(role, e):
                ln.rows1(role, e)
            def s_sqrt2(role, e):
                ln.sqrt(role, e)
                ln.rows2(role, e)
            def s_bcast(role, e):
                ln.bcast(role, e)
            def mk_apply(r0, r1):
                def s_a(role, e):
                    ln.apply(role, e, r0, r1)
                return s_a
            def mk_h128(g0):
                def s_h(role, e):
                    emit_h128_grp(role, e, dst_ln, zbuf, T, ln.sv_apply, st8, g0)
                return s_h
            def s_bfly(role, e):
                cell["sv_zf"] = emit_butterflies(role, e, dst_ln, zbuf, T,
                                                 st8["sa_z"][15])
            return [s_load, mk_q(0), mk_q(1), mk_q(2), mk_q(3), s_rows1,
                    s_sqrt2, s_bcast, mk_apply(0, 4), mk_apply(4, 8),
                    mk_h128(0), mk_h128(4), mk_h128(8), mk_h128(12), s_bfly]

        FRONT_SCHED = {0: [0, 1], 1: [2, 3], 2: [4, 5], 3: [6],
                       4: [7, 8], 5: [9, 10], 6: [11, 12], 7: [13, 14]}

        def pa(role, e):
            n_t = len(A_TILES)
            lns1 = {}
            lns2 = {}
            st8s = {}
            cells = {}
            t_x1s = {}
            t_ln2s = {}

            def new_front(ti):
                t0, T = A_TILES[ti]
                par = ti % 2
                src = x_t[:, par]
                ln = LNPipe(stat, rows, tsub, src, lnb, T)
                lns1[ti] = ln
                st8s[ti] = {"sp_z": {}, "sa_z": {}}
                cells[ti] = {}

                def xg_fn():
                    if ti >= 2:
                        return [("dve", lns2[ti - 2].sv_apply),
                                ("st", t_x1s[ti - 2])]
                    return []

                def prep_g0_fn():
                    if ti >= 2:
                        return [("pe", lns2[ti - 2].pe_marks[3])]
                    if ti == 1:
                        return [("pe", lns1[0].pe_marks[3])]
                    return []

                def mm_g0_fn():
                    if ti >= 2:
                        return [("dve", lns2[ti - 2].sv_apply)]
                    if ti == 1:
                        return [("dve", lns1[0].sv_apply)]
                    return []

                return make_front(src, lnb, fwb, T, t0, x_img, xg_fn, ln,
                                  st8s[ti], cells[ti], mm_g0_fn, prep_g0_fn)

            for st_ in new_front(0):
                st_(role, e)

            for ti in range(n_t):
                t0, T = A_TILES[ti]
                par = ti % 2
                nxt_steps = new_front(ti + 1) if ti + 1 < n_t else None
                ln2 = LNPipe(stat, rows, tsub, x_t[:, par], h_b[:, 48:64], T)
                lns2[ti] = ln2
                cellg = {}

                def l2hook(role_, e_, c8, sv_res, _n=nxt_steps, _ti=ti,
                           _ln2=ln2):
                    if _n is not None:
                        for si in FRONT_SCHED[c8]:
                            _n[si](role_, e_)
                        if c8 == 4:
                            _ln2.statprep(role_, e_, 0, guards=[
                                ("pe", lns1[_ti + 1].pe_marks[3])])
                        elif c8 == 7:
                            _ln2.statmm(role_, e_, 0, pe_guards=[
                                ("act", st8s[_ti + 1]["sa_z"][15]),
                                ("dve", lns1[_ti + 1].sv_apply)])

                def l1hook(role_, e_, it, _ti=ti):
                    if _ti == 0 or it > 8:
                        return
                    lp = lns2[_ti - 1]
                    pt0, pT = A_TILES[_ti - 1]
                    if it == 0:
                        lp.statprep(role_, e_, 1)
                        lp.statmm(role_, e_, 1)
                    elif it == 1:
                        lp.statprep(role_, e_, 2)
                        lp.statmm(role_, e_, 2)
                    elif it == 2:
                        lp.statprep(role_, e_, 3)
                        lp.statmm(role_, e_, 3)
                    elif it == 3:
                        lp.rows1(role_, e_)
                    elif it == 4:
                        lp.sqrt(role_, e_)
                        lp.rows2(role_, e_)
                    elif it == 5:
                        lp.bcast(role_, e_)
                    elif it == 6:
                        lp.apply(role_, e_, 0, 4)
                    elif it == 7:
                        lp.apply(role_, e_, 4, 8)
                    elif it == 8:
                        t = dinc("st")
                        if role_ == "sync":
                            e_.wait_ge(S["dve"], lp.sv_apply)
                            e_.dma_start(ln2s[:, :, pt0:pt0 + pT],
                                         h_b[:, 48:64, 0:pT]
                                         ).then_inc(S["st"], 16)
                        t_ln2s[_ti - 1] = t
                        cellg["mg"] = ("st", t)

                emit_ffn(role, e, fwb, h_b, silu_b, x_t[:, par], wug_s,
                         wdn_s, T, cells[ti]["sv_zf"], gate_stash=(ti <= 1),
                         loop1_hook=l1hook, loop2_hook=l2hook,
                         mid_guard_cell=cellg)
                t = dinc("st")
                if role == "sync":
                    e.wait_ge(S["dve"], C["dve"])
                    e.dma_start(x1s[:, :, t0:t0 + T], x_t[:, par, :, 0:T]
                                ).then_inc(S["st"], 16)
                t_x1s[ti] = t
            ti = n_t - 1
            t0, T = A_TILES[ti]
            lp = lns2[ti]
            lp.statprep(role, e, 0, guards=[("pe", lns2[ti - 1].pe_marks[3])])
            lp.statmm(role, e, 0, pe_guards=[("dve", lns2[ti - 1].sv_apply)])
            for q in range(1, 4):
                lp.statprep(role, e, q)
                lp.statmm(role, e, q)
            lp.rows1(role, e)
            lp.sqrt(role, e)
            lp.rows2(role, e)
            lp.bcast(role, e)
            lp.apply(role, e, 0, 8)
            t = dinc("st")
            if role == "sync":
                e.wait_ge(S["dve"], lp.sv_apply)
                e.dma_start(ln2s[:, :, t0:t0 + T], h_b[:, 48:64, 0:T]
                            ).then_inc(S["st"], 16)
            t_ln2s[ti] = t
            barrier(role, e)

        if want("a"):
            run_phase(pa)
        aes.close()
        if not want("bc"):
            return nc

        bces = ExitStack()
        scan_out = bces.enter_context(
            nc.sbuf_tensor("scan_out", [128, 16, WIN], BF16))
        bces2 = ExitStack()
        win_s = bces2.enter_context(
            nc.sbuf_tensor("win_s", [128, 2, 16, 512], BF16))
        lntile = bces2.enter_context(
            nc.sbuf_tensor("lntile", [128, 2, 16, 512], BF16))
        proj = bces2.enter_context(nc.sbuf_tensor("proj", [128, 4, WIN], F32))

        def pbc(role, e):
            sv_scan_prev = 0
            pe_eg_marks = {}
            for eg in range(4):
                wpar = eg % 2
                wsem = "gww" if wpar == 0 else "gw0"
                t_winq = (wsem, dinc(wsem))
                if role == "gpsimd":
                    if eg >= 2:
                        e.wait_ge(S["pe"], pe_eg_marks[eg - 2])
                    e.dma_start(win_s[:, wpar],
                                win_img[:, :, eg * 512:(eg + 1) * 512]
                                ).then_inc(S[wsem], 16)
                sa_ev = {}
                sp_mark = {}
                for tti, (t0, T) in enumerate(A_TILES):
                    pb = tti % 2
                    ln_sem = "ldln0" if pb == 0 else "ldln1"
                    t_lnt = (ln_sem, dinc(ln_sem))
                    if role == "sync":
                        if tti >= 2:
                            e.wait_ge(S["pe"], sp_mark[(tti - 2, 3)])
                        elif eg >= 1:
                            e.wait_ge(S["pe"], pe_eg_done)
                        e.dma_start(lntile[:, pb, :, 0:T], ln2s[:, :, t0:t0 + T]
                                    ).then_inc(S[ln_sem], 16)
                    for el in range(4):
                        et = eg * 4 + el
                        if role == "tensor":
                            if tti == 0 and el == 0:
                                e.wait_ge(S[t_winq[0]], t_winq[1])
                            if el == 0:
                                e.wait_ge(S[t_lnt[0]], t_lnt[1])
                            if tti >= 1:
                                e.wait_ge(S["act"], sa_ev[(tti - 1, el)])
                            for dg in range(16):
                                ins = e.matmul(
                                    ps8[:, el, 0:T],
                                    win_s[:, wpar, dg, el * 128:(el + 1) * 128],
                                    lntile[:, pb, dg, 0:T],
                                    start=(dg == 0), stop=(dg == 15))
                                if dg == 15:
                                    ins.then_inc(S["pe"], 1)
                        sp_mark[(tti, el)] = inc("pe")
                        if role == "scalar":
                            e.wait_ge(S["pe"], sp_mark[(tti, el)])
                            if eg >= 1 and tti == 0:
                                e.wait_ge(S["dve"], sv_scan_prev)
                            e.activation(proj[:, el, t0:t0 + T], ps8[:, el, 0:T],
                                         AF.Copy, scale=omd[:, et:et + 1]
                                         ).then_inc(S["act"], 1)
                        sa_ev[(tti, el)] = inc("act")
                pe_eg_done = sp_mark[(4, 3)]
                pe_eg_marks[eg] = pe_eg_done
                for el in range(4):
                    et = eg * 4 + el
                    if role == "vector":
                        e.wait_ge(S["act"], sa_ev[(4, el)])
                        e.tensor_tensor_scan(
                            scan_out[:, et, :],
                            dmat[:, et:et + 1].broadcast_to([128, WIN]),
                            proj[:, el, :], 0.0, ALU.mult, ALU.add
                        ).then_inc(S["dve"], 1)
                    sv_scan_prev = inc("dve")
                if dbg_on and eg == 0:
                    t_dbg = dinc("st")
                    if role == "sync":
                        e.wait_ge(S["dve"], sv_scan_prev)
                        e.dma_start(dbg_proj[:, :, :], proj[:]).then_inc(S["st"], 16)
                    if role == "scalar":
                        e.wait_ge(S["st"], t_dbg)
            if dbg_on:
                t_dbg2 = dinc("st")
                if role == "sync":
                    e.dma_start(dbg_scan[:, :, :], scan_out[:]).then_inc(S["st"], 16)
            barrier(role, e)

        if want("bc"):
            run_phase(pbc)
        bces2.close()
        if not want("d1"):
            bces.close()
            return nc

        d1es = ExitStack()
        wout_s = d1es.enter_context(nc.sbuf_tensor("wout_s", [128, 16, 2048], BF16))
        xd_t = d1es.enter_context(nc.sbuf_tensor("xd_t", [128, 2, 16, 512], F32))

        def pd1(role, e):
            wq_sems = ("gww", "gw0", "gw1", "gwb0")
            t_wq = {}
            for qr in range(4):
                t_wq[qr] = (wq_sems[qr], dinc(wq_sems[qr]))
                if role == "gpsimd":
                    e.dma_start(wout_s[:, :, qr * 512:(qr + 1) * 512],
                                wout_img[:, :, qr * 512:(qr + 1) * 512]
                                ).then_inc(S[wq_sems[qr]], 16)
            t_x1 = {}
            t_x2s = {}

            def load_x1(tti):
                t0, T = D_TILES[tti]
                par = tti % 2
                t_x1[tti] = dinc("ld")
                if role == "sync":
                    if tti >= 2:
                        e.wait_ge(S["st"], t_x2s[tti - 2])
                    e.dma_start(xd_t[:, par, :, 0:T], x1s[:, :, t0:t0 + T]
                                ).then_inc(S["ld"], 16)

            load_x1(0)
            load_x1(1)
            for tti, (t0, T) in enumerate(D_TILES):
                par = tti % 2
                sv_x2 = {}
                for dt in range(16):
                    slot = dt % 4
                    if role == "tensor":
                        if tti == 0 and dt == 0:
                            for qr in range(4):
                                e.wait_ge(S[t_wq[qr][0]], t_wq[qr][1])
                        if dt >= 4 or tti > 0:
                            w = sv_x2[dt - 4] if dt >= 4 else prev_sv_x2[12 + slot]
                            e.wait_ge(S["dve"], w)
                        for et in range(16):
                            ins = e.matmul(ps8[:, slot, 0:T],
                                           wout_s[:, et, dt * 128:(dt + 1) * 128],
                                           scan_out[:, et, t0:t0 + T],
                                           start=(et == 0), stop=(et == 15))
                            if et == 15:
                                ins.then_inc(S["pe"], 1)
                    sp = inc("pe")
                    if role == "vector":
                        e.wait_ge(S["pe"], sp)
                        if dt == 0:
                            e.wait_ge(S["ld"], t_x1[tti])
                        e.scalar_tensor_tensor(xd_t[:, par, dt, 0:T],
                                               ps8[:, slot, 0:T],
                                               biasc[:, dt:dt + 1],
                                               xd_t[:, par, dt, 0:T],
                                               ALU.add, ALU.add
                                               ).then_inc(S["dve"], 1)
                    sv_x2[dt] = inc("dve")
                prev_sv_x2 = sv_x2
                t_x2s[tti] = dinc("st")
                if role == "sync":
                    e.wait_ge(S["dve"], sv_x2[15])
                    e.dma_start(x2s[:, :, tti * 512:(tti + 1) * 512],
                                xd_t[:, par, :, 0:T]).then_inc(S["st"], 16)
                if tti + 2 < len(D_TILES):
                    load_x1(tti + 2)
            barrier(role, e)

        if want("d1"):
            run_phase(pd1)
        d1es.close()
        bces.close()
        if not want("d2"):
            return nc

        d2es = ExitStack()
        x_t2 = d2es.enter_context(nc.sbuf_tensor("x_t2", [128, 2, 16, 512], F32))
        lnb2 = d2es.enter_context(nc.sbuf_tensor("lnb2", [128, 16, 512], BF16))
        fwb2 = d2es.enter_context(nc.sbuf_tensor("fwb2", [128, 16, 512], BF16))
        h_b2 = d2es.enter_context(nc.sbuf_tensor("h_b2", [128, 64, 512], BF16))
        stat2 = d2es.enter_context(nc.sbuf_tensor("stat2", [128, 2, 4, 512], BF16))
        tsub2 = d2es.enter_context(nc.sbuf_tensor("tsub2", [128, 1, 512], F32))
        silu2 = d2es.enter_context(nc.sbuf_tensor("silu2", [128, 2, 512], BF16))
        rows2 = d2es.enter_context(nc.sbuf_tensor("rows2", [1, 3, 512], F32))
        wug_s2 = d2es.enter_context(nc.sbuf_tensor("wug_s2", [128, 2, 2, 2048], BF16))
        wdn_s2 = d2es.enter_context(nc.sbuf_tensor("wdn_s2", [128, 2, 8, 256], BF16))

        def pd2(role, e):
            n_t = len(D_TILES)
            lns3 = {}
            st8s = {}
            cells = {}
            t_out = {}
            sv_res_t = {}

            def new_front(ti):
                t0o = ti * 512
                T = 512
                par = ti % 2
                src = x_t2[:, par]
                ln = LNPipe(stat2, rows2, tsub2, src, lnb2, T)
                lns3[ti] = ln
                st8s[ti] = {"sp_z": {}, "sa_z": {}}
                cells[ti] = {}

                def xg_fn():
                    if ti >= 2:
                        return [("dve", sv_res_t[ti - 2]),
                                ("st", t_out[ti - 2])]
                    return []

                def prep_g0_fn():
                    if ti >= 1:
                        return [("pe", lns3[ti - 1].pe_marks[3])]
                    return []

                def mm_g0_fn():
                    if ti >= 1:
                        return [("act", st8s[ti - 1]["sa_z"][15]),
                                ("dve", lns3[ti - 1].sv_apply)]
                    return []

                return make_front(src, lnb2, fwb2, T, t0o, x2s, xg_fn, ln,
                                  st8s[ti], cells[ti], mm_g0_fn, prep_g0_fn)

            for st_ in new_front(0):
                st_(role, e)

            for tti in range(n_t):
                t0o, T = tti * 512, 512
                par = tti % 2
                nxt_steps = new_front(tti + 1) if tti + 1 < n_t else None

                def l2hook(role_, e_, c8, sv_res, _n=nxt_steps):
                    if _n is not None:
                        for si in FRONT_SCHED[c8]:
                            _n[si](role_, e_)

                sv_res_t[tti] = emit_ffn(
                    role, e, fwb2, h_b2, silu2, x_t2[:, par], wug_s2,
                    wdn_s2, T, cells[tti]["sv_zf"], loop2_hook=l2hook)
                t_out[tti] = dinc("st")
                if role == "sync":
                    e.wait_ge(S["dve"], sv_res_t[tti])
                    e.dma_start(out_img[:, :, t0o:t0o + T],
                                x_t2[:, par, :, 0:T]).then_inc(S["st"], 16)
            if role == "sync":
                e.wait_ge(S["st"], C["st"])

        if want("d2"):
            run_phase(pd2)
        d2es.close()

    return nc


_CACHE = {}


def prep_in_maps(L0_attn_x, L0_attn_W_in, L0_attn_W_out, L0_attn_decay,
                 L0_attn_bias, L0_ffn_SU_ug, L0_ffn_SU_dp, L0_ffn_Wscale_ug,
                 L0_ffn_Wscale_dp, L0_ffn_W_up, L0_ffn_W_gate, L0_ffn_W_down):
    f32 = np.float32
    x = np.asarray(L0_attn_x, f32)
    W_in = np.asarray(L0_attn_W_in, f32)
    W_out = np.asarray(L0_attn_W_out, f32)
    decay = np.asarray(L0_attn_decay, f32)
    bias = np.asarray(L0_attn_bias, f32)
    SU_ug = np.asarray(L0_ffn_SU_ug, f32)
    SU_dp = np.asarray(L0_ffn_SU_dp, f32)
    Ws_ug = np.asarray(L0_ffn_Wscale_ug, f32)
    Ws_dp = np.asarray(L0_ffn_Wscale_dp, f32)
    W_up = np.asarray(L0_ffn_W_up, f32)
    W_gate = np.asarray(L0_ffn_W_gate, f32)
    W_down = np.asarray(L0_ffn_W_down, f32)

    ac = np.ascontiguousarray
    wup_img = ac(W_up.T.reshape(16, 128, 64, 128).transpose(2, 1, 0, 3)
                 .reshape(64, 128, 2048))
    wgt_img = ac(W_gate.T.reshape(16, 128, 64, 128).transpose(2, 1, 0, 3)
                 .reshape(64, 128, 2048))
    wdn_img = ac(W_down.T.reshape(64, 128, 16, 128).transpose(2, 1, 0, 3)
                 .reshape(16, 128, 8192))
    win_img = ac(W_in.T.reshape(16, 128, 2048).transpose(1, 0, 2))
    wout_img = ac(W_out.T.reshape(16, 128, 2048).transpose(1, 0, 2))
    dec_img = ac(decay.reshape(16, 128).T)
    bias_img = ac(bias.reshape(16, 128).T)
    wsu_img = ac(Ws_ug[:8192].reshape(64, 128).T)
    wsg_img = ac(Ws_ug[8192:].reshape(64, 128).T)
    wdp_img = ac(Ws_dp.reshape(16, 128).T)
    su_img = ac(SU_ug.reshape(16, 128).T)
    sdp_img = ac(SU_dp.reshape(64, 128).T)
    H = np.array([[1.0]], dtype=f32)
    for _ in range(7):
        H = np.block([[H, H], [H, -H]]).astype(f32)
    h128_img = ac(H)

    shared = dict(wup_img=wup_img, wgt_img=wgt_img, wdn_img=wdn_img,
                  win_img=win_img, wout_img=wout_img, dec_img=dec_img,
                  bias_img=bias_img, wsu_img=wsu_img, wsg_img=wsg_img,
                  wdp_img=wdp_img, su_img=su_img, sdp_img=sdp_img,
                  h128_img=h128_img)

    in_maps = []
    for c in range(NCORES):
        b, k = divmod(c, 4)
        lo = k * 2048 - LB
        winx = np.zeros((WIN, 2048), f32)
        if lo < 0:
            winx[LB:] = x[b, 0:2048]
        else:
            winx[:] = x[b, lo:lo + WIN]
        ximg = ac(winx.T.reshape(16, 128, WIN).transpose(1, 0, 2))
        m = dict(shared)
        m["x_img"] = ximg
        in_maps.append(m)
    return in_maps


def gather_out(res):
    f32 = np.float32
    out = np.empty((2, 8192, 2048), f32)
    for c in range(NCORES):
        b, k = divmod(c, 4)
        arr = np.asarray(res.results[c]["out_img"])
        out[b, k * 2048:(k + 1) * 2048, :] = arr.transpose(1, 0, 2).reshape(
            2048, 2048).T
    return out


def kernel(**inputs):
    in_maps = prep_in_maps(**inputs)
    if "nc" not in _CACHE:
        _CACHE["nc"] = build()
    res = run_bass_kernel_spmd(_CACHE["nc"], in_maps, core_ids=list(range(NCORES)))
    return gather_out(res)


# version 12
# speedup vs baseline: 1.1104x; 1.1104x over previous
import os
"""Fused megakernel for mk_macaron_d1 (retnet mixer + hadamard-SwiGLU FFN).

One raw-Bass program per NeuronCore, single run_bass_kernel_spmd launch.
Sharding: 16384 tokens -> 8 cores x 2048 tokens (+256-token lookback for the
EMA scan; decay=sigmoid(0.1*N(0,1)) so d^256 < 1e-38 — exact at fp32/fp64).

Layout: feature-major ([features on partitions, tokens on free]).
FWHT-2048 on activations: per-tile (H128*diag(SU)) TensorE matmul, then 4
cross-tile butterfly stages on DVE (bf16).  FWHT-8192 folded into W_down on
device once (f32 butterflies + H4-folded H128 matmuls), along with SU_dp and
the up-scale; all other element scales folded into per-partition psum-evac
scales.  EMA scan = one tensor_tensor_scan per 128-channel tile over the full
2304-token window.  bf16 matmul operands / f32 accumulation / f32 elementwise
(this problem's stated precision bar).
"""
import numpy as np
from contextlib import ExitStack

import concourse.bass as bass
import concourse.mybir as mybir
from concourse.bass_utils import run_bass_kernel_spmd

F32 = mybir.dt.float32
BF16 = mybir.dt.bfloat16
AF = mybir.ActivationFunctionType
ALU = mybir.AluOpType

NCORES = 8
D, I = 2048, 8192
DG, IG = 16, 64
LB = 256
WIN = 2304
WOUT = 2048
A_TILES = [(0, 512), (512, 512), (1024, 512), (1536, 512), (2048, 256)]
D_TILES = [(256, 512), (768, 512), (1280, 512), (1792, 512)]
C_Z = float(1.0 / (64.0 * np.sqrt(2048.0)))
EPS = 1e-5
ROLES = ("sync", "gpsimd", "vector", "scalar", "tensor")
H4SIGN = [[1, 1, 1, 1], [1, -1, 1, -1], [1, 1, -1, -1], [1, -1, -1, 1]]


def build(upto=None):
    import os
    if upto is None:
        upto = os.environ.get("K_UPTO", "d2")
    stages = ["p0", "pre", "a", "bc", "d1", "d2"]
    lim = stages.index(upto)

    def want(name):
        return stages.index(name) <= lim

    nc = bass.Bass("TRN2", target_bir_lowering=False)

    x_img = nc.dram_tensor("x_img", [128, 16, WIN], F32, kind="ExternalInput")
    wup_img = nc.dram_tensor("wup_img", [64, 128, 2048], F32, kind="ExternalInput")
    wgt_img = nc.dram_tensor("wgt_img", [64, 128, 2048], F32, kind="ExternalInput")
    wdn_img = nc.dram_tensor("wdn_img", [16, 128, 8192], F32, kind="ExternalInput")
    win_img = nc.dram_tensor("win_img", [128, 16, 2048], F32, kind="ExternalInput")
    wout_img = nc.dram_tensor("wout_img", [128, 16, 2048], F32, kind="ExternalInput")
    dec_img = nc.dram_tensor("dec_img", [128, 16], F32, kind="ExternalInput")
    bias_img = nc.dram_tensor("bias_img", [128, 16], F32, kind="ExternalInput")
    wsu_img = nc.dram_tensor("wsu_img", [128, 64], F32, kind="ExternalInput")
    wsg_img = nc.dram_tensor("wsg_img", [128, 64], F32, kind="ExternalInput")
    wdp_img = nc.dram_tensor("wdp_img", [128, 16], F32, kind="ExternalInput")
    su_img = nc.dram_tensor("su_img", [128, 16], F32, kind="ExternalInput")
    sdp_img = nc.dram_tensor("sdp_img", [128, 64], F32, kind="ExternalInput")
    h128_img = nc.dram_tensor("h128_img", [128, 128], F32, kind="ExternalInput")
    out_img = nc.dram_tensor("out_img", [128, 16, WOUT], F32, kind="ExternalOutput")
    _k = dict(kind="ExternalOutput") if os.environ.get("K_DBG") else {}
    wup2 = nc.dram_tensor("wup2", [64, 128, 2048], BF16, **_k)
    wgt2 = nc.dram_tensor("wgt2", [64, 128, 2048], BF16, **_k)
    wdn2 = nc.dram_tensor("wdn2", [8, 8, 128, 2048], BF16, **_k)
    x1s = nc.dram_tensor("x1s", [128, 16, WIN], F32, **_k)
    ln2s = nc.dram_tensor("ln2s", [128, 16, WIN], BF16, **_k)
    x2s = nc.dram_tensor("x2s", [128, 16, WOUT], F32, **_k)
    dbg_on = bool(os.environ.get("K_DBG"))
    if dbg_on:
        dbg_proj = nc.dram_tensor("dbg_proj", [128, 4, WIN], F32, kind="ExternalOutput")
        dbg_scan = nc.dram_tensor("dbg_scan", [128, 16, WIN], BF16, kind="ExternalOutput")

    C = {k: 0 for k in ("ld", "st", "sc", "gw", "gpc", "pe", "act", "dve",
                        "bar", "lddn0", "lddn1", "ldln0", "ldln1",
                        "sc0", "sc1", "gw0", "gw1", "gwb", "gwb0", "gwb1",
                        "gww")}

    es = ExitStack()
    with es:
        S = {k: es.enter_context(nc.semaphore(k)) for k in C}
        ps8 = es.enter_context(nc.psum_tensor("ps8", [128, 8, 512], F32))
        dmat = es.enter_context(nc.sbuf_tensor("dmat", [128, 16], F32))
        omd = es.enter_context(nc.sbuf_tensor("omd", [128, 16], F32))
        wdp05 = es.enter_context(nc.sbuf_tensor("wdp05", [128, 16], F32))
        biasc = es.enter_context(nc.sbuf_tensor("biasc", [128, 16], F32))
        sg64 = es.enter_context(nc.sbuf_tensor("sg64", [128, 64], F32))
        dwnsc = es.enter_context(nc.sbuf_tensor("dwnsc", [128, 64], F32))
        hsub = es.enter_context(nc.sbuf_tensor("hsub", [128, 16, 128], BF16))
        Hb = es.enter_context(nc.sbuf_tensor("Hb", [128, 128], BF16))
        Hnb = es.enter_context(nc.sbuf_tensor("Hnb", [128, 128], BF16))
        ones_colb = es.enter_context(nc.sbuf_tensor("ones_colb", [128, 1], BF16))
        ones1f = es.enter_context(nc.sbuf_tensor("ones1f", [1, 128], F32))
        block = es.enter_context(nc.Block())

        def inc(sem):
            C[sem] += 1
            return C[sem]

        def dinc(sem):
            C[sem] += 16
            return C[sem]

        def run_phase(fn):
            base = dict(C)
            ends = []
            for role in ROLES:
                C.clear()
                C.update(base)

                def _f(eng, _r=role):
                    fn(_r, eng)

                getattr(block, role)(_f)
                ends.append(dict(C))
            for e2 in ends:
                assert e2 == ends[0], f"counter divergence {ends}"
            C.clear()
            C.update(ends[0])

        def barrier(role, e):
            if role == "sync":
                for k in ("ld", "st", "lddn0", "lddn1", "ldln0", "ldln1"):
                    if C[k]:
                        e.wait_ge(S[k], C[k])
            elif role == "scalar":
                for k in ("sc", "sc0", "sc1", "act"):
                    if C[k]:
                        e.wait_ge(S[k], C[k])
            elif role == "gpsimd":
                for k in ("gw", "gw0", "gw1", "gwb", "gwb0", "gwb1", "gww",
                          "gpc"):
                    if C[k]:
                        e.wait_ge(S[k], C[k])
            elif role == "vector":
                if C["dve"]:
                    e.wait_ge(S["dve"], C["dve"])
            elif role == "tensor":
                if C["pe"]:
                    e.wait_ge(S["pe"], C["pe"])
            C["bar"] += 5
            tgt = C["bar"]
            e.nop().then_inc(S["bar"], 1)
            e.wait_ge(S["bar"], tgt)

        p0es = ExitStack()
        dec_r = p0es.enter_context(nc.sbuf_tensor("dec_r", [128, 16], F32))
        wsu_r = p0es.enter_context(nc.sbuf_tensor("wsu_r", [128, 64], F32))
        wsg_r = p0es.enter_context(nc.sbuf_tensor("wsg_r", [128, 64], F32))
        wdp_r = p0es.enter_context(nc.sbuf_tensor("wdp_r", [128, 16], F32))
        su_r = p0es.enter_context(nc.sbuf_tensor("su_r", [128, 16], F32))
        sdp_r = p0es.enter_context(nc.sbuf_tensor("sdp_r", [128, 64], F32))
        h128f = p0es.enter_context(nc.sbuf_tensor("h128f", [128, 128], F32))

        def p0(role, e):
            loads = [(dec_r, dec_img), (wsu_r, wsu_img), (wsg_r, wsg_img),
                     (wdp_r, wdp_img), (su_r, su_img), (sdp_r, sdp_img),
                     (h128f, h128_img), (biasc, bias_img)]
            for dst, src in loads:
                t = dinc("ld")
                if role == "sync":
                    e.dma_start(dst[:], src[:, :]).then_inc(S["ld"], 16)
            t_ld = C["ld"]
            if role == "gpsimd":
                e.memset(ones_colb[:], 1.0)
                e.memset(ones1f[:], 1.0).then_inc(S["gpc"], 1)
            C["gpc"] += 1
            sa0 = inc("act")
            if role == "scalar":
                e.wait_ge(S["ld"], t_ld)
                e.activation(dmat[:], dec_r[:], AF.Sigmoid).then_inc(S["act"], 1)
            sv0 = None
            if True:
                if role == "vector":
                    e.wait_ge(S["act"], sa0)
                    e.tensor_scalar(omd[:], dmat[:], -1.0, 1.0, ALU.mult, ALU.add)
                    e.tensor_scalar(wdp05[:], wdp_r[:], 0.5, None, ALU.mult)
                    e.tensor_scalar(sg64[:], wsg_r[:], 64.0 * C_Z, None, ALU.mult)
                    e.tensor_tensor(dwnsc[:], wsu_r[:], sdp_r[:], ALU.mult)
                    e.tensor_scalar(dwnsc[:], dwnsc[:],
                                    0.7071067811865476 * C_Z, None, ALU.mult)
                    e.tensor_copy(Hb[:], h128f[:])
                    e.tensor_scalar(Hnb[:], h128f[:], -1.0, None, ALU.mult)
                    for g in range(16):
                        ins = e.tensor_scalar(hsub[:, g], h128f[:],
                                              su_r[:, g:g + 1], None, ALU.mult)
                        if g == 15:
                            ins.then_inc(S["dve"], 1)
                sv0 = inc("dve")
            barrier(role, e)

        run_phase(p0)
        p0es.close()
        if not want("pre"):
            return nc

        prees = ExitStack()
        wdn_in = prees.enter_context(
            nc.sbuf_tensor("wdn_in", [128, 2, 64, 128], F32))
        wdn_pg = prees.enter_context(nc.sbuf_tensor("wdn_pg", [128, 64, 128], BF16))
        wdn_pg2 = prees.enter_context(
            nc.sbuf_tensor("wdn_pg2", [128, 64, 128], BF16))
        wdn_b = prees.enter_context(
            nc.sbuf_tensor("wdn_b", [128, 2, 64, 128], BF16))
        wdn_o = prees.enter_context(nc.sbuf_tensor("wdn_o", [128, 64, 256], BF16))

        def pre(role, e):
            p0_dve = C["dve"]
            dve_hist = {}
            pe_hist = {}
            prev_act_done = 0
            t_chunk = {}
            sa_marks = {}
            for s_ in range(16):
                hd = s_ % 2
                par = s_ % 2
                gsem = "gw0" if par == 0 else "gw1"
                t_chunk[s_] = (gsem, dinc(gsem))
                if role == "sync":
                    if s_ >= 2:
                        e.wait_ge(S["dve"], dve_hist[s_ - 2])
                    e.dma_start(wdn_in[:, par],
                                wdn_img[s_].rearrange("p (o d) -> p o d", o=64)
                                ).then_inc(S[gsem], 16)
                groups = [] if s_ == 0 else ([s_ - 1] if s_ < 15 else [14, 15])
                for grp in groups:
                    bsem = "gwb0" if grp % 2 == 0 else "gwb1"
                    bval = C[bsem]
                    C[bsem] += 32
                    if role == "gpsimd":
                        if grp >= 2:
                            e.wait_ge(S[bsem], bval)
                        e.dma_start(wup2[4 * grp:4 * grp + 4],
                                    wup_img[4 * grp:4 * grp + 4]
                                    ).then_inc(S[bsem], 16)
                        e.dma_start(wgt2[4 * grp:4 * grp + 4],
                                    wgt_img[4 * grp:4 * grp + 4]
                                    ).then_inc(S[bsem], 16)
                if role == "vector":
                    e.wait_ge(S[gsem], t_chunk[s_][1])
                    if s_ >= 2:
                        e.wait_ge(S["pe"], pe_hist[s_ - 2])
                    win_ = wdn_in[:, par]
                    e.tensor_tensor(wdn_pg[:, 0:32], win_[:, 0:32],
                                    win_[:, 32:64], ALU.add)
                    e.tensor_tensor(wdn_pg[:, 32:64], win_[:, 0:32],
                                    win_[:, 32:64], ALU.subtract)
                    for (src, dst, a, dlt) in ((wdn_pg, wdn_pg2, 2, 16),
                                               (wdn_pg2, wdn_pg, 4, 8)):
                        vs = src.rearrange("p (a b c) d -> p a b c d", a=a, b=2, c=dlt)
                        vd = dst.rearrange("p (a b c) d -> p a b c d", a=a, b=2, c=dlt)
                        e.tensor_tensor(vd[:, :, 0], vs[:, :, 0], vs[:, :, 1], ALU.add)
                        e.tensor_tensor(vd[:, :, 1], vs[:, :, 0], vs[:, :, 1],
                                        ALU.subtract)
                    vs = wdn_pg.rearrange("p (a b c) d -> p a b c d", a=8, b=2, c=4)
                    vd = wdn_b[:, par].rearrange("p (a b c) d -> p a b c d",
                                                 a=8, b=2, c=4)
                    e.tensor_tensor(vd[:, :, 0], vs[:, :, 0], vs[:, :, 1], ALU.add)
                    e.tensor_tensor(vd[:, :, 1], vs[:, :, 0], vs[:, :, 1],
                                    ALU.subtract).then_inc(S["dve"], 1)
                dve_hist[s_] = inc("dve")
                for g2 in range(16):
                    q = g2 % 2
                    if role == "tensor":
                        if g2 == 0:
                            e.wait_ge(S["dve"], dve_hist[s_])
                            if s_ == 0:
                                e.wait_ge(S["dve"], p0_dve)
                        if g2 >= 2 or s_ > 0:
                            w = prev_act_done if g2 < 2 else sa_marks[g2 - 2]
                            e.wait_ge(S["act"], w)
                        for j in range(4):
                            for k in range(4):
                                lhs = Hb if H4SIGN[j][k] > 0 else Hnb
                                ins = e.matmul(ps8[:, q * 4 + j, 0:128], lhs[:],
                                               wdn_b[:, par, 4 * g2 + k, :],
                                               start=(k == 0), stop=(k == 3))
                                if j == 3 and k == 3:
                                    ins.then_inc(S["pe"], 1)
                    sp = inc("pe")
                    if role == "scalar":
                        e.wait_ge(S["pe"], sp)
                        e.activation(
                            wdn_o[:, 4 * g2:4 * g2 + 4, hd * 128:(hd + 1) * 128],
                            ps8[:, q * 4:q * 4 + 4, 0:128], AF.Copy
                            ).then_inc(S["act"], 1)
                    sa_marks[g2] = inc("act")
                pe_hist[s_] = C["pe"]
                prev_act_done = C["act"]
                if hd == 1:
                    cc = s_ // 2
                    for p8 in range(8):
                        t = dinc("sc")
                        if role == "scalar":
                            e.dma_start(
                                wdn2[cc, p8].rearrange("p (o d) -> p o d", o=8),
                                wdn_o[:, 8 * p8:8 * p8 + 8, :]
                                ).then_inc(S["sc"], 16)
                    t_sc = C["sc"]
                    if role == "scalar":
                        e.wait_ge(S["sc"], t_sc)
            barrier(role, e)

        if want("pre"):
            run_phase(pre)
        prees.close()
        if not want("a"):
            return nc

        class LNPipe:

            def __init__(self, stat, rows, tsub, src, dst, T):
                self.stat, self.rows, self.tsub = stat, rows, tsub
                self.src, self.dst, self.T = src, dst, T
                self.pe_marks = {}
                self.sv_stats = {}

            def statprep(self, role, e, q, guards=()):
                T = self.T
                if role == "vector":
                    for sem, val in guards:
                        e.wait_ge(S[sem], val)
                    if q >= 1:
                        e.wait_ge(S["pe"], self.pe_marks[q - 1])
                    e.tensor_copy(self.stat[:, 0, :, 0:T],
                                  self.src[:, 4 * q:4 * q + 4, 0:T])
                    e.tensor_tensor(self.stat[:, 1, :, 0:T],
                                    self.src[:, 4 * q:4 * q + 4, 0:T],
                                    self.src[:, 4 * q:4 * q + 4, 0:T], ALU.mult
                                    ).then_inc(S["dve"], 1)
                self.sv_stats[q] = inc("dve")

            def statmm(self, role, e, q, pe_guards=()):
                T = self.T
                if role == "tensor":
                    e.wait_ge(S["dve"], self.sv_stats[q])
                    for sem, val in pe_guards:
                        e.wait_ge(S[sem], val)
                    for j in range(4):
                        e.matmul(ps8[0:1, 7, 0:T], ones_colb[:, 0:1],
                                 self.stat[:, 0, j, 0:T],
                                 start=(q == 0 and j == 0),
                                 stop=(q == 3 and j == 3))
                        ins = e.matmul(ps8[0:1, 0, 0:T], ones_colb[:, 0:1],
                                       self.stat[:, 1, j, 0:T],
                                       start=(q == 0 and j == 0),
                                       stop=(q == 3 and j == 3))
                    ins.then_inc(S["pe"], 1)
                self.pe_marks[q] = inc("pe")

            def rows1(self, role, e):
                T, rows = self.T, self.rows
                if role == "vector":
                    e.wait_ge(S["pe"], self.pe_marks[3])
                    e.tensor_scalar(rows[0:1, 0, 0:T], ps8[0:1, 7, 0:T],
                                    1.0 / 2048.0, None, ALU.mult)
                    e.tensor_scalar(rows[0:1, 1, 0:T], ps8[0:1, 0, 0:T],
                                    1.0 / 2048.0, None, ALU.mult)
                    e.tensor_tensor(rows[0:1, 2, 0:T], rows[0:1, 0, 0:T],
                                    rows[0:1, 0, 0:T], ALU.mult)
                    e.tensor_tensor(rows[0:1, 1, 0:T], rows[0:1, 1, 0:T],
                                    rows[0:1, 2, 0:T], ALU.subtract)
                    e.tensor_scalar(rows[0:1, 2, 0:T], rows[0:1, 1, 0:T],
                                    EPS, None, ALU.add).then_inc(S["dve"], 1)
                self.sv_var = inc("dve")

            def sqrt(self, role, e):
                T, rows = self.T, self.rows
                if role == "scalar":
                    e.wait_ge(S["dve"], self.sv_var)
                    e.activation(rows[0:1, 1, 0:T], rows[0:1, 2, 0:T], AF.Sqrt
                                 ).then_inc(S["act"], 1)
                self.sa_sq = inc("act")

            def rows2(self, role, e):
                T, rows = self.T, self.rows
                if role == "vector":
                    e.wait_ge(S["act"], self.sa_sq)
                    e.reciprocal(rows[0:1, 2, 0:T], rows[0:1, 1, 0:T])
                    e.tensor_tensor(rows[0:1, 1, 0:T], rows[0:1, 0, 0:T],
                                    rows[0:1, 2, 0:T], ALU.mult
                                    ).then_inc(S["dve"], 1)
                self.sv_rows = inc("dve")

            def bcast(self, role, e):
                T, rows = self.T, self.rows
                if role == "tensor":
                    e.wait_ge(S["dve"], self.sv_rows)
                    e.matmul(ps8[:, 7, 0:T], ones1f[0:1, :], rows[0:1, 1, 0:T],
                             start=True, stop=True)
                    e.matmul(ps8[:, 0, 0:T], ones1f[0:1, :], rows[0:1, 2, 0:T],
                             start=True, stop=True).then_inc(S["pe"], 1)
                self.sp_bc = inc("pe")

            def apply(self, role, e, r0, r1, dst=None):
                T = self.T
                dst = self.dst if dst is None else dst
                if role == "vector":
                    if r0 == 0:
                        e.wait_ge(S["pe"], self.sp_bc)
                    mb = ps8[:, 7:8, 0:T].broadcast_to([128, 2, T])
                    rb = ps8[:, 0:1, 0:T].broadcast_to([128, 2, T])
                    for q8 in range(r0, r1):
                        e.tensor_tensor(self.tsub[:, 0:1, 0:T],
                                        self.src[:, 2 * q8:2 * q8 + 1, 0:T],
                                        mb[:, 0:1], ALU.subtract)
                        e.tensor_tensor(dst[:, 2 * q8:2 * q8 + 1, 0:T],
                                        self.tsub[:, 0:1, 0:T], rb[:, 0:1],
                                        ALU.mult)
                        e.tensor_tensor(self.tsub[:, 0:1, 0:T],
                                        self.src[:, 2 * q8 + 1:2 * q8 + 2, 0:T],
                                        mb[:, 0:1], ALU.subtract)
                        ins = e.tensor_tensor(
                            dst[:, 2 * q8 + 1:2 * q8 + 2, 0:T],
                            self.tsub[:, 0:1, 0:T], rb[:, 0:1], ALU.mult)
                        if q8 == r1 - 1:
                            ins.then_inc(S["dve"], 1)
                self.sv_apply = inc("dve")

        def emit_h128_grp(role, e, lnb, fwb, T, sv_lnapply, st8, g0):
            for g in range(g0, g0 + 4):
                slot = 7 if g % 2 == 0 else 0
                if role == "tensor":
                    if g == 0:
                        e.wait_ge(S["dve"], sv_lnapply)
                    if g >= 2:
                        e.wait_ge(S["act"], st8["sa_z"][g - 2])
                    e.matmul(ps8[:, slot, 0:T], hsub[:, g], lnb[:, g, 0:T],
                             start=True, stop=True).then_inc(S["pe"], 1)
                st8["sp_z"][g] = inc("pe")
                if role == "scalar":
                    e.wait_ge(S["pe"], st8["sp_z"][g])
                    e.activation(fwb[:, g, 0:T], ps8[:, slot, 0:T], AF.Copy
                                 ).then_inc(S["act"], 1)
                st8["sa_z"][g] = inc("act")

        def emit_butterflies(role, e, lnb, fwb, T, sa_z15):
            if role == "vector":
                e.wait_ge(S["act"], sa_z15)
                for i, (src, dst, dlt) in enumerate(
                        ((fwb, lnb, 8), (lnb, fwb, 4), (fwb, lnb, 2), (lnb, fwb, 1))):
                    a = 16 // (2 * dlt)
                    vs = src.rearrange("p (a b c) t -> p a b c t", a=a, b=2, c=dlt)
                    vd = dst.rearrange("p (a b c) t -> p a b c t", a=a, b=2, c=dlt)
                    e.tensor_tensor(vd[:, :, 0, :, 0:T], vs[:, :, 0, :, 0:T],
                                    vs[:, :, 1, :, 0:T], ALU.add)
                    ins = e.tensor_tensor(vd[:, :, 1, :, 0:T], vs[:, :, 0, :, 0:T],
                                          vs[:, :, 1, :, 0:T], ALU.subtract)
                    if i == 3:
                        ins.then_inc(S["dve"], 1)
            return inc("dve")

        def emit_ffn(role, e, zf, h, silu_b, x_t, wug_s, wdn_s, T,
                     sv_zf, gate_stash=False, loop1_hook=None, loop2_hook=None,
                     mid_guard_cell=None):
            sp_up = {}
            sp_gt = {}
            sa_silu = {}
            sv_mid = {}
            t_slab = {}
            gwb_tot = {"gwb0": C["gwb0"], "gwb1": C["gwb1"]}

            def slab_sem(j):
                return "sc0" if j % 2 == 0 else "sc1"

            def slab_load(j):
                sem = slab_sem(j)
                t1 = dinc(sem)
                t2 = dinc(sem)
                if role == "scalar":
                    if gate_stash:
                        g = j // 4
                        bs = "gwb0" if g % 2 == 0 else "gwb1"
                        e.wait_ge(S[bs], min(32 * (g // 2 + 1), gwb_tot[bs]))
                    pbj = j % 2
                    e.dma_start(wug_s[:, 0, pbj, :], wup2[j]
                                ).then_inc(S[sem], 16)
                    e.dma_start(wug_s[:, 1, pbj, :], wgt2[j]
                                ).then_inc(S[sem], 16)
                t_slab[j] = t2

            slab_load(0)
            slab_load(1)
            for it in range(64):
                pb = it % 2
                s_up = 1 if pb == 0 else 3
                s_gt = 2 if pb == 0 else 4
                if role == "tensor":
                    if it == 0:
                        e.wait_ge(S["dve"], sv_zf)
                    e.wait_ge(S[slab_sem(it)], t_slab[it])
                    if it >= 2:
                        e.wait_ge(S["dve"], sv_mid[it - 2])
                        e.wait_ge(S["act"], sa_silu[it - 2])
                    for dg in range(16):
                        ins = e.matmul(ps8[:, s_up, 0:T], wug_s[:, 0, pb,
                                       dg * 128:(dg + 1) * 128], zf[:, dg, 0:T],
                                       start=(dg == 0), stop=(dg == 15))
                        if dg == 15:
                            ins.then_inc(S["pe"], 1)
                sp_up[it] = inc("pe")
                if role == "tensor":
                    for dg in range(16):
                        ins = e.matmul(ps8[:, s_gt, 0:T], wug_s[:, 1, pb,
                                       dg * 128:(dg + 1) * 128], zf[:, dg, 0:T],
                                       start=(dg == 0), stop=(dg == 15))
                        if dg == 15:
                            ins.then_inc(S["pe"], 1)
                sp_gt[it] = inc("pe")
                if role == "scalar":
                    e.wait_ge(S["pe"], sp_gt[it])
                if it + 2 < 64:
                    slab_load(it + 2)
                if role == "scalar":
                    e.wait_ge(S["pe"], sp_gt[it])
                    if it >= 2:
                        e.wait_ge(S["dve"], sv_mid[it - 2])
                    e.activation(silu_b[:, pb, 0:T], ps8[:, s_gt, 0:T], AF.Silu,
                                 scale=sg64[:, it:it + 1]).then_inc(S["act"], 1)
                sa_silu[it] = inc("act")
                if role == "vector":
                    e.wait_ge(S["act"], sa_silu[it])
                    if it == 48 and mid_guard_cell and "mg" in mid_guard_cell:
                        mg = mid_guard_cell["mg"]
                        e.wait_ge(S[mg[0]], mg[1])
                    e.scalar_tensor_tensor(h[:, it, 0:T], ps8[:, s_up, 0:T],
                                           dwnsc[:, it:it + 1],
                                           silu_b[:, pb, 0:T],
                                           ALU.mult, ALU.mult
                                           ).then_inc(S["dve"], 1)
                sv_mid[it] = inc("dve")
                if loop1_hook is not None:
                    loop1_hook(role, e, it)
            pp_done = {}
            sv_res = {}
            t_dnld = {}
            for c8 in range(8):
                dt0, dt1 = 2 * c8, 2 * c8 + 1
                sA, sB = (5, 6) if c8 % 2 == 0 else (1, 2)
                for p8 in range(8):
                    idx = c8 * 8 + p8
                    pc = idx % 2
                    dn_sem = "lddn0" if pc == 0 else "lddn1"
                    if role == "sync":
                        if idx >= 2:
                            e.wait_ge(S["pe"], pp_done[idx - 2])
                    t_dnld[idx] = (dn_sem, dinc(dn_sem))
                    if role == "sync":
                        e.dma_start(
                            wdn_s[:, pc],
                            wdn2[c8, p8].rearrange("p (o d) -> p o d", o=8)
                            ).then_inc(S[dn_sem], 16)
                    if role == "tensor":
                        if c8 == 0 and p8 == 0:
                            e.wait_ge(S["dve"], sv_mid[63])
                        if c8 >= 2 and p8 == 0:
                            e.wait_ge(S["dve"], sv_res[c8 - 2])
                        e.wait_ge(S[t_dnld[idx][0]], t_dnld[idx][1])
                        for ol in range(8):
                            o = p8 * 8 + ol
                            e.matmul(ps8[:, sA, 0:T], wdn_s[:, pc, ol, 0:128],
                                     h[:, o, 0:T], start=(o == 0), stop=(o == 63))
                            ins = e.matmul(ps8[:, sB, 0:T],
                                           wdn_s[:, pc, ol, 128:256],
                                           h[:, o, 0:T], start=(o == 0),
                                           stop=(o == 63))
                            if ol == 7:
                                ins.then_inc(S["pe"], 1)
                    pp_done[idx] = inc("pe")
                if role == "vector":
                    e.wait_ge(S["pe"], pp_done[c8 * 8 + 7])
                    e.scalar_tensor_tensor(x_t[:, dt0, 0:T], ps8[:, sA, 0:T],
                                           wdp05[:, dt0:dt0 + 1],
                                           x_t[:, dt0, 0:T], ALU.mult, ALU.add)
                    e.scalar_tensor_tensor(x_t[:, dt1, 0:T], ps8[:, sB, 0:T],
                                           wdp05[:, dt1:dt1 + 1],
                                           x_t[:, dt1, 0:T], ALU.mult, ALU.add
                                           ).then_inc(S["dve"], 1)
                sv_res[c8] = inc("dve")
                if loop2_hook is not None:
                    loop2_hook(role, e, c8, sv_res)
            return sv_res[7]

        aes = ExitStack()
        x_t = aes.enter_context(nc.sbuf_tensor("x_t", [128, 2, 16, 512], F32))
        lnb = aes.enter_context(nc.sbuf_tensor("lnb", [128, 16, 512], BF16))
        fwb = aes.enter_context(nc.sbuf_tensor("fwb", [128, 16, 512], BF16))
        h_b = aes.enter_context(nc.sbuf_tensor("h_b", [128, 64, 512], BF16))
        stat = aes.enter_context(nc.sbuf_tensor("stat", [128, 2, 4, 512], BF16))
        tsub = aes.enter_context(nc.sbuf_tensor("tsub", [128, 1, 512], F32))
        silu_b = aes.enter_context(nc.sbuf_tensor("silu_b", [128, 2, 512], BF16))
        rows = aes.enter_context(nc.sbuf_tensor("rows", [1, 3, 512], F32))
        wug_s = aes.enter_context(nc.sbuf_tensor("wug_s", [128, 2, 2, 2048], BF16))
        wdn_s = aes.enter_context(nc.sbuf_tensor("wdn_s", [128, 2, 8, 256], BF16))

        def make_front(src, dst_ln, zbuf, T, t0, img, xg_fn, ln, st8, cell,
                       statmm_g0_fn, statprep_g0_fn):
            def s_load(role, e):
                t = dinc("ld")
                if role == "sync":
                    for sem, val in xg_fn():
                        e.wait_ge(S[sem], val)
                    e.dma_start(src[:, :, 0:T], img[:, :, t0:t0 + T]
                                ).then_inc(S["ld"], 16)
                cell["t_x"] = t
            def mk_q(q):
                def s_q(role, e):
                    g = [("ld", cell["t_x"])] + (statprep_g0_fn() if q == 0
                                                 else [])
                    ln.statprep(role, e, q, guards=g)
                    ln.statmm(role, e, q,
                              pe_guards=(statmm_g0_fn() if q == 0 else []))
                return s_q
            def s_rows1(role, e):
                ln.rows1(role, e)
            def s_sqrt2(role, e):
                ln.sqrt(role, e)
                ln.rows2(role, e)
            def s_bcast(role, e):
                ln.bcast(role, e)
            def mk_apply(r0, r1):
                def s_a(role, e):
                    ln.apply(role, e, r0, r1)
                return s_a
            def mk_h128(g0):
                def s_h(role, e):
                    emit_h128_grp(role, e, dst_ln, zbuf, T, ln.sv_apply, st8, g0)
                return s_h
            def s_bfly(role, e):
                cell["sv_zf"] = emit_butterflies(role, e, dst_ln, zbuf, T,
                                                 st8["sa_z"][15])
            return [s_load, mk_q(0), mk_q(1), mk_q(2), mk_q(3), s_rows1,
                    s_sqrt2, s_bcast, mk_apply(0, 4), mk_apply(4, 8),
                    mk_h128(0), mk_h128(4), mk_h128(8), mk_h128(12), s_bfly]

        FRONT_SCHED = {0: [0, 1], 1: [2, 3], 2: [4, 5], 3: [6],
                       4: [7, 8], 5: [9, 10], 6: [11, 12], 7: [13, 14]}

        def pa(role, e):
            n_t = len(A_TILES)
            lns1 = {}
            lns2 = {}
            st8s = {}
            cells = {}
            t_x1s = {}
            t_ln2s = {}

            def new_front(ti):
                t0, T = A_TILES[ti]
                par = ti % 2
                src = x_t[:, par]
                ln = LNPipe(stat, rows, tsub, src, lnb, T)
                lns1[ti] = ln
                st8s[ti] = {"sp_z": {}, "sa_z": {}}
                cells[ti] = {}

                def xg_fn():
                    if ti >= 2:
                        return [("dve", lns2[ti - 2].sv_apply),
                                ("st", t_x1s[ti - 2])]
                    return []

                def prep_g0_fn():
                    if ti >= 2:
                        return [("pe", lns2[ti - 2].pe_marks[3])]
                    if ti == 1:
                        return [("pe", lns1[0].pe_marks[3])]
                    return []

                def mm_g0_fn():
                    if ti >= 2:
                        return [("dve", lns2[ti - 2].sv_apply)]
                    if ti == 1:
                        return [("dve", lns1[0].sv_apply)]
                    return []

                return make_front(src, lnb, fwb, T, t0, x_img, xg_fn, ln,
                                  st8s[ti], cells[ti], mm_g0_fn, prep_g0_fn)

            for st_ in new_front(0):
                st_(role, e)

            for ti in range(n_t):
                t0, T = A_TILES[ti]
                par = ti % 2
                nxt_steps = new_front(ti + 1) if ti + 1 < n_t else None
                ln2 = LNPipe(stat, rows, tsub, x_t[:, par], h_b[:, 48:64], T)
                lns2[ti] = ln2
                cellg = {}

                def l2hook(role_, e_, c8, sv_res, _n=nxt_steps, _ti=ti,
                           _ln2=ln2):
                    if _n is not None:
                        for si in FRONT_SCHED[c8]:
                            _n[si](role_, e_)
                        if c8 == 4:
                            _ln2.statprep(role_, e_, 0, guards=[
                                ("pe", lns1[_ti + 1].pe_marks[3])])
                        elif c8 == 7:
                            _ln2.statmm(role_, e_, 0, pe_guards=[
                                ("act", st8s[_ti + 1]["sa_z"][15]),
                                ("dve", lns1[_ti + 1].sv_apply)])

                def l1hook(role_, e_, it, _ti=ti):
                    if _ti == 0 or it > 8:
                        return
                    lp = lns2[_ti - 1]
                    pt0, pT = A_TILES[_ti - 1]
                    if it == 0:
                        lp.statprep(role_, e_, 1)
                        lp.statmm(role_, e_, 1)
                    elif it == 1:
                        lp.statprep(role_, e_, 2)
                        lp.statmm(role_, e_, 2)
                    elif it == 2:
                        lp.statprep(role_, e_, 3)
                        lp.statmm(role_, e_, 3)
                    elif it == 3:
                        lp.rows1(role_, e_)
                    elif it == 4:
                        lp.sqrt(role_, e_)
                        lp.rows2(role_, e_)
                    elif it == 5:
                        lp.bcast(role_, e_)
                    elif it == 6:
                        lp.apply(role_, e_, 0, 4)
                    elif it == 7:
                        lp.apply(role_, e_, 4, 8)
                    elif it == 8:
                        t = dinc("st")
                        if role_ == "sync":
                            e_.wait_ge(S["dve"], lp.sv_apply)
                            e_.dma_start(ln2s[:, :, pt0:pt0 + pT],
                                         h_b[:, 48:64, 0:pT]
                                         ).then_inc(S["st"], 16)
                        t_ln2s[_ti - 1] = t
                        cellg["mg"] = ("st", t)

                emit_ffn(role, e, fwb, h_b, silu_b, x_t[:, par], wug_s,
                         wdn_s, T, cells[ti]["sv_zf"], gate_stash=(ti <= 1),
                         loop1_hook=l1hook, loop2_hook=l2hook,
                         mid_guard_cell=cellg)
                t = dinc("st")
                if role == "sync":
                    e.wait_ge(S["dve"], C["dve"])
                    e.dma_start(x1s[:, :, t0:t0 + T], x_t[:, par, :, 0:T]
                                ).then_inc(S["st"], 16)
                t_x1s[ti] = t
            ti = n_t - 1
            t0, T = A_TILES[ti]
            lp = lns2[ti]
            lp.statprep(role, e, 0, guards=[("pe", lns2[ti - 1].pe_marks[3])])
            lp.statmm(role, e, 0, pe_guards=[("dve", lns2[ti - 1].sv_apply)])
            for q in range(1, 4):
                lp.statprep(role, e, q)
                lp.statmm(role, e, q)
            lp.rows1(role, e)
            lp.sqrt(role, e)
            lp.rows2(role, e)
            lp.bcast(role, e)
            lp.apply(role, e, 0, 8)
            t = dinc("st")
            if role == "sync":
                e.wait_ge(S["dve"], lp.sv_apply)
                e.dma_start(ln2s[:, :, t0:t0 + T], h_b[:, 48:64, 0:T]
                            ).then_inc(S["st"], 16)
            t_ln2s[ti] = t
            barrier(role, e)

        if want("a"):
            run_phase(pa)
        aes.close()
        if not want("bc"):
            return nc

        bces = ExitStack()
        scan_out = bces.enter_context(
            nc.sbuf_tensor("scan_out", [128, 16, WIN], BF16))
        bces2 = ExitStack()
        win_s = bces2.enter_context(
            nc.sbuf_tensor("win_s", [128, 2, 16, 512], BF16))
        lntile = bces2.enter_context(
            nc.sbuf_tensor("lntile", [128, 2, 16, 512], BF16))
        proj = bces2.enter_context(nc.sbuf_tensor("proj", [128, 4, WIN], F32))

        def pbc(role, e):
            sv_scan_prev = 0
            pe_eg_marks = {}
            for eg in range(4):
                wpar = eg % 2
                wsem = "gww" if wpar == 0 else "gw0"
                t_winq = (wsem, dinc(wsem))
                if role == "gpsimd":
                    if eg >= 2:
                        e.wait_ge(S["pe"], pe_eg_marks[eg - 2])
                    e.dma_start(win_s[:, wpar],
                                win_img[:, :, eg * 512:(eg + 1) * 512]
                                ).then_inc(S[wsem], 16)
                sa_ev = {}
                sp_mark = {}
                for tti, (t0, T) in enumerate(A_TILES):
                    pb = tti % 2
                    ln_sem = "ldln0" if pb == 0 else "ldln1"
                    t_lnt = (ln_sem, dinc(ln_sem))
                    if role == "sync":
                        if tti >= 2:
                            e.wait_ge(S["pe"], sp_mark[(tti - 2, 3)])
                        elif eg >= 1:
                            e.wait_ge(S["pe"], pe_eg_done)
                        e.dma_start(lntile[:, pb, :, 0:T], ln2s[:, :, t0:t0 + T]
                                    ).then_inc(S[ln_sem], 16)
                    for el in range(4):
                        et = eg * 4 + el
                        if role == "tensor":
                            if tti == 0 and el == 0:
                                e.wait_ge(S[t_winq[0]], t_winq[1])
                            if el == 0:
                                e.wait_ge(S[t_lnt[0]], t_lnt[1])
                            if tti >= 1:
                                e.wait_ge(S["act"], sa_ev[(tti - 1, el)])
                            for dg in range(16):
                                ins = e.matmul(
                                    ps8[:, el, 0:T],
                                    win_s[:, wpar, dg, el * 128:(el + 1) * 128],
                                    lntile[:, pb, dg, 0:T],
                                    start=(dg == 0), stop=(dg == 15))
                                if dg == 15:
                                    ins.then_inc(S["pe"], 1)
                        sp_mark[(tti, el)] = inc("pe")
                        if role == "scalar":
                            e.wait_ge(S["pe"], sp_mark[(tti, el)])
                            if eg >= 1 and tti == 0:
                                e.wait_ge(S["dve"], sv_scan_prev)
                            e.activation(proj[:, el, t0:t0 + T], ps8[:, el, 0:T],
                                         AF.Copy, scale=omd[:, et:et + 1]
                                         ).then_inc(S["act"], 1)
                        sa_ev[(tti, el)] = inc("act")
                pe_eg_done = sp_mark[(4, 3)]
                pe_eg_marks[eg] = pe_eg_done
                for el in range(4):
                    et = eg * 4 + el
                    if role == "vector":
                        e.wait_ge(S["act"], sa_ev[(4, el)])
                        e.tensor_tensor_scan(
                            scan_out[:, et, :],
                            dmat[:, et:et + 1].broadcast_to([128, WIN]),
                            proj[:, el, :], 0.0, ALU.mult, ALU.add
                        ).then_inc(S["dve"], 1)
                    sv_scan_prev = inc("dve")
                if dbg_on and eg == 0:
                    t_dbg = dinc("st")
                    if role == "sync":
                        e.wait_ge(S["dve"], sv_scan_prev)
                        e.dma_start(dbg_proj[:, :, :], proj[:]).then_inc(S["st"], 16)
                    if role == "scalar":
                        e.wait_ge(S["st"], t_dbg)
            if dbg_on:
                t_dbg2 = dinc("st")
                if role == "sync":
                    e.dma_start(dbg_scan[:, :, :], scan_out[:]).then_inc(S["st"], 16)
            barrier(role, e)

        if want("bc"):
            run_phase(pbc)
        bces2.close()
        if not want("d1"):
            bces.close()
            return nc

        d1es = ExitStack()
        wout_s = d1es.enter_context(nc.sbuf_tensor("wout_s", [128, 16, 2048], BF16))
        xd_t = d1es.enter_context(nc.sbuf_tensor("xd_t", [128, 2, 16, 512], F32))

        def pd1(role, e):
            wq_sems = ("gww", "gw0", "gw1", "gwb0")
            t_wq = {}
            for qr in range(4):
                t_wq[qr] = (wq_sems[qr], dinc(wq_sems[qr]))
                if role == "gpsimd":
                    e.dma_start(wout_s[:, :, qr * 512:(qr + 1) * 512],
                                wout_img[:, :, qr * 512:(qr + 1) * 512]
                                ).then_inc(S[wq_sems[qr]], 16)
            t_x1 = {}
            t_x2s = {}

            def load_x1(tti):
                t0, T = D_TILES[tti]
                par = tti % 2
                t_x1[tti] = dinc("ld")
                if role == "sync":
                    if tti >= 2:
                        e.wait_ge(S["st"], t_x2s[tti - 2])
                    e.dma_start(xd_t[:, par, :, 0:T], x1s[:, :, t0:t0 + T]
                                ).then_inc(S["ld"], 16)

            load_x1(0)
            load_x1(1)
            for tti, (t0, T) in enumerate(D_TILES):
                par = tti % 2
                sv_x2 = {}
                for dt in range(16):
                    slot = dt % 4
                    if role == "tensor":
                        if tti == 0 and dt == 0:
                            for qr in range(4):
                                e.wait_ge(S[t_wq[qr][0]], t_wq[qr][1])
                        if dt >= 4 or tti > 0:
                            w = sv_x2[dt - 4] if dt >= 4 else prev_sv_x2[12 + slot]
                            e.wait_ge(S["dve"], w)
                        for et in range(16):
                            ins = e.matmul(ps8[:, slot, 0:T],
                                           wout_s[:, et, dt * 128:(dt + 1) * 128],
                                           scan_out[:, et, t0:t0 + T],
                                           start=(et == 0), stop=(et == 15))
                            if et == 15:
                                ins.then_inc(S["pe"], 1)
                    sp = inc("pe")
                    if role == "vector":
                        e.wait_ge(S["pe"], sp)
                        if dt == 0:
                            e.wait_ge(S["ld"], t_x1[tti])
                        e.scalar_tensor_tensor(xd_t[:, par, dt, 0:T],
                                               ps8[:, slot, 0:T],
                                               biasc[:, dt:dt + 1],
                                               xd_t[:, par, dt, 0:T],
                                               ALU.add, ALU.add
                                               ).then_inc(S["dve"], 1)
                    sv_x2[dt] = inc("dve")
                prev_sv_x2 = sv_x2
                t_x2s[tti] = dinc("st")
                if role == "sync":
                    e.wait_ge(S["dve"], sv_x2[15])
                    e.dma_start(x2s[:, :, tti * 512:(tti + 1) * 512],
                                xd_t[:, par, :, 0:T]).then_inc(S["st"], 16)
                if tti + 2 < len(D_TILES):
                    load_x1(tti + 2)
            barrier(role, e)

        if want("d1"):
            run_phase(pd1)
        d1es.close()
        bces.close()
        if not want("d2"):
            return nc

        d2es = ExitStack()
        x_t2 = d2es.enter_context(nc.sbuf_tensor("x_t2", [128, 2, 16, 512], F32))
        lnb2 = d2es.enter_context(nc.sbuf_tensor("lnb2", [128, 16, 512], BF16))
        fwb2 = d2es.enter_context(nc.sbuf_tensor("fwb2", [128, 16, 512], BF16))
        h_b2 = d2es.enter_context(nc.sbuf_tensor("h_b2", [128, 64, 512], BF16))
        stat2 = d2es.enter_context(nc.sbuf_tensor("stat2", [128, 2, 4, 512], BF16))
        tsub2 = d2es.enter_context(nc.sbuf_tensor("tsub2", [128, 1, 512], F32))
        silu2 = d2es.enter_context(nc.sbuf_tensor("silu2", [128, 2, 512], BF16))
        rows2 = d2es.enter_context(nc.sbuf_tensor("rows2", [1, 3, 512], F32))
        wug_s2 = d2es.enter_context(nc.sbuf_tensor("wug_s2", [128, 2, 2, 2048], BF16))
        wdn_s2 = d2es.enter_context(nc.sbuf_tensor("wdn_s2", [128, 2, 8, 256], BF16))

        def pd2(role, e):
            n_t = len(D_TILES)
            lns3 = {}
            st8s = {}
            cells = {}
            t_out = {}
            sv_res_t = {}

            def new_front(ti):
                t0o = ti * 512
                T = 512
                par = ti % 2
                src = x_t2[:, par]
                ln = LNPipe(stat2, rows2, tsub2, src, lnb2, T)
                lns3[ti] = ln
                st8s[ti] = {"sp_z": {}, "sa_z": {}}
                cells[ti] = {}

                def xg_fn():
                    if ti >= 2:
                        return [("dve", sv_res_t[ti - 2]),
                                ("st", t_out[ti - 2])]
                    return []

                def prep_g0_fn():
                    if ti >= 1:
                        return [("pe", lns3[ti - 1].pe_marks[3])]
                    return []

                def mm_g0_fn():
                    if ti >= 1:
                        return [("act", st8s[ti - 1]["sa_z"][15]),
                                ("dve", lns3[ti - 1].sv_apply)]
                    return []

                return make_front(src, lnb2, fwb2, T, t0o, x2s, xg_fn, ln,
                                  st8s[ti], cells[ti], mm_g0_fn, prep_g0_fn)

            for st_ in new_front(0):
                st_(role, e)

            for tti in range(n_t):
                t0o, T = tti * 512, 512
                par = tti % 2
                nxt_steps = new_front(tti + 1) if tti + 1 < n_t else None

                def l2hook(role_, e_, c8, sv_res, _n=nxt_steps):
                    if _n is not None:
                        for si in FRONT_SCHED[c8]:
                            _n[si](role_, e_)

                sv_res_t[tti] = emit_ffn(
                    role, e, fwb2, h_b2, silu2, x_t2[:, par], wug_s2,
                    wdn_s2, T, cells[tti]["sv_zf"], loop2_hook=l2hook)
                t_out[tti] = dinc("st")
                if role == "sync":
                    e.wait_ge(S["dve"], sv_res_t[tti])
                    e.dma_start(out_img[:, :, t0o:t0o + T],
                                x_t2[:, par, :, 0:T]).then_inc(S["st"], 16)
            if role == "sync":
                e.wait_ge(S["st"], C["st"])

        if want("d2"):
            run_phase(pd2)
        d2es.close()

    return nc


_CACHE = {}


def prep_in_maps(L0_attn_x, L0_attn_W_in, L0_attn_W_out, L0_attn_decay,
                 L0_attn_bias, L0_ffn_SU_ug, L0_ffn_SU_dp, L0_ffn_Wscale_ug,
                 L0_ffn_Wscale_dp, L0_ffn_W_up, L0_ffn_W_gate, L0_ffn_W_down):
    f32 = np.float32
    x = np.asarray(L0_attn_x, f32)
    W_in = np.asarray(L0_attn_W_in, f32)
    W_out = np.asarray(L0_attn_W_out, f32)
    decay = np.asarray(L0_attn_decay, f32)
    bias = np.asarray(L0_attn_bias, f32)
    SU_ug = np.asarray(L0_ffn_SU_ug, f32)
    SU_dp = np.asarray(L0_ffn_SU_dp, f32)
    Ws_ug = np.asarray(L0_ffn_Wscale_ug, f32)
    Ws_dp = np.asarray(L0_ffn_Wscale_dp, f32)
    W_up = np.asarray(L0_ffn_W_up, f32)
    W_gate = np.asarray(L0_ffn_W_gate, f32)
    W_down = np.asarray(L0_ffn_W_down, f32)

    ac = np.ascontiguousarray
    wup_img = ac(W_up.T.reshape(16, 128, 64, 128).transpose(2, 1, 0, 3)
                 .reshape(64, 128, 2048))
    wgt_img = ac(W_gate.T.reshape(16, 128, 64, 128).transpose(2, 1, 0, 3)
                 .reshape(64, 128, 2048))
    wdn_img = ac(W_down.T.reshape(64, 128, 16, 128).transpose(2, 1, 0, 3)
                 .reshape(16, 128, 8192))
    win_img = ac(W_in.T.reshape(16, 128, 2048).transpose(1, 0, 2))
    wout_img = ac(W_out.T.reshape(16, 128, 2048).transpose(1, 0, 2))
    dec_img = ac(decay.reshape(16, 128).T)
    bias_img = ac(bias.reshape(16, 128).T)
    wsu_img = ac(Ws_ug[:8192].reshape(64, 128).T)
    wsg_img = ac(Ws_ug[8192:].reshape(64, 128).T)
    wdp_img = ac(Ws_dp.reshape(16, 128).T)
    su_img = ac(SU_ug.reshape(16, 128).T)
    sdp_img = ac(SU_dp.reshape(64, 128).T)
    H = np.array([[1.0]], dtype=f32)
    for _ in range(7):
        H = np.block([[H, H], [H, -H]]).astype(f32)
    h128_img = ac(H)

    shared = dict(wup_img=wup_img, wgt_img=wgt_img, wdn_img=wdn_img,
                  win_img=win_img, wout_img=wout_img, dec_img=dec_img,
                  bias_img=bias_img, wsu_img=wsu_img, wsg_img=wsg_img,
                  wdp_img=wdp_img, su_img=su_img, sdp_img=sdp_img,
                  h128_img=h128_img)

    in_maps = []
    for c in range(NCORES):
        b, k = divmod(c, 4)
        lo = k * 2048 - LB
        winx = np.zeros((WIN, 2048), f32)
        if lo < 0:
            winx[LB:] = x[b, 0:2048]
        else:
            winx[:] = x[b, lo:lo + WIN]
        ximg = ac(winx.T.reshape(16, 128, WIN).transpose(1, 0, 2))
        m = dict(shared)
        m["x_img"] = ximg
        in_maps.append(m)
    return in_maps


def gather_out(res):
    f32 = np.float32
    out = np.empty((2, 8192, 2048), f32)
    for c in range(NCORES):
        b, k = divmod(c, 4)
        arr = np.asarray(res.results[c]["out_img"])
        out[b, k * 2048:(k + 1) * 2048, :] = arr.transpose(1, 0, 2).reshape(
            2048, 2048).T
    return out


def kernel(**inputs):
    in_maps = prep_in_maps(**inputs)
    if "nc" not in _CACHE:
        _CACHE["nc"] = build()
    res = run_bass_kernel_spmd(_CACHE["nc"], in_maps, core_ids=list(range(NCORES)))
    return gather_out(res)
